# version 10
# baseline (speedup 1.0000x reference)
import math
import numpy as np
from contextlib import ExitStack

import concourse.bass as bass
import concourse.mybir as mybir
from concourse.bass_utils import run_bass_kernel_spmd

F32 = mybir.dt.float32
BF16 = mybir.dt.bfloat16
FP8 = mybir.dt.float8e4
ALU = mybir.AluOpType
ACT = mybir.ActivationFunctionType
AX = mybir.AxisListType.X

P = 128
D = 1024
H = 4096
N = 4096
NCH = 32
DC = 8
HT = 32
NB = 256
NBLK = 16
NSUB = 2
NT = NBLK * NSUB
INV_D = 1.0 / 1024.0
INV_H = 1.0 / 4096.0
EPS = 1e-5
ISQ = 1.0 / math.sqrt(128.0)


def build_nc():
    nc = bass.Bass("TRN2", target_bir_lowering=False, detect_race_conditions=False)

    dJ = nc.dram_tensor("J", [N, D], F32, kind="ExternalInput")
    dwqT = nc.dram_tensor("wqT", [D, 128], F32, kind="ExternalInput")
    dwkT = nc.dram_tensor("wkT", [D, 128], F32, kind="ExternalInput")
    dw1T = nc.dram_tensor("w1T", [D, H], F32, kind="ExternalInput")
    dw2T = nc.dram_tensor("w2T", [H, D], F32, kind="ExternalInput")
    dw1n = nc.dram_tensor("w1n", [H, D], F32, kind="ExternalInput")
    dw2n = nc.dram_tensor("w2n", [D, H], F32, kind="ExternalInput")
    dgB = nc.dram_tensor("gB", [P, D], F32, kind="ExternalInput")
    dbB = nc.dram_tensor("bB", [P, D], F32, kind="ExternalInput")
    db2B = nc.dram_tensor("b2B", [P, D], F32, kind="ExternalInput")
    db1P = nc.dram_tensor("b1P", [P, 32], F32, kind="ExternalInput")
    dout = nc.dram_tensor("out", [N, D], F32, kind="ExternalOutput")
    dxnat = nc.dram_tensor("xnat", [N, D], BF16, kind="Internal")
    da2 = nc.dram_tensor("a2scr", [D], F32, kind="Internal")

    ctx = ExitStack()
    sb = lambda n, s, d: ctx.enter_context(nc.sbuf_tensor(n, s, d))
    sem = lambda n: ctx.enter_context(nc.semaphore(n))

    with ctx:
        S1T = sb("S1T", [P, DC * H], FP8)
        S2T = sb("S2T", [P, HT * D], FP8)
        QT = sb("QT", [P, N], BF16)
        Msb = sb("Msb", [P, D], BF16)
        alpha1P = sb("alpha1P", [P, 32], F32)
        AL1 = sb("AL1", [P, 32], F32)
        AL2p = sb("AL2p", [P, 32], F32)
        AL2q = sb("AL2q", [P, 16], F32)
        AL2r = sb("AL2r", [P, 8], F32)
        AL2s = sb("AL2s", [P, 8], F32)
        b1P = sb("b1Ps", [P, 32], F32)
        onesF = sb("onesF", [1, P], F32)
        junk1 = sb("junk1", [P, 8], F32)
        jctr = [0]
        SU = sb("SU", [P, 32], F32)
        SQ = sb("SQ", [P, 32], F32)
        MU = sb("MU", [P, 32], F32)
        E2E = sb("E2E", [P, 32], F32)
        NEGMU = sb("NEGMU", [P, 32], F32)
        VAR = sb("VAR", [P, 32], F32)
        REC = sb("REC", [P, 32], F32)
        RS = sb("RS", [P, 32], F32)
        NMS = sb("NMS", [P, 32], F32)
        RS2 = sb("RS2", [P, 32], F32)
        NMS2 = sb("NMS2", [P, 32], F32)
        dumpA = sb("dumpA", [P, D], F32)
        XTD = sb("XTD", [P, 2 * 2048], BF16)
        NTI1 = sb("NTI1", [P, 32], mybir.dt.int32)
        NTI2 = sb("NTI2", [P, 32], mybir.dt.int32)
        NTIY = sb("NTIY", [P, 32], mybir.dt.int32)
        NTYa = sb("NTYa", [P, 32], F32)
        NTYb = sb("NTYb", [P, 32], F32)
        NTA = sb("NTA", [P, 32], F32)
        NTB = sb("NTB", [P, 32], F32)
        NTH = sb("NTH", [P, 32], F32)

        s_j = [sem(f"s_j{k}") for k in range(10)]
        s_xw = [sem(f"s_xw{k}") for k in range(3)]
        s_sq = sem("s_sq"); s_vrec = sem("s_vrec"); s_rs = sem("s_rs")
        s_vnm = sem("s_vnm"); s_xn = sem("s_xn")
        s_wt = [sem(f"s_wt{k}") for k in range(2)]
        s_sg = sem("s_sg")
        s_wn = [sem(f"s_wn{k}") for k in range(2)]
        s_ab = sem("s_ab"); s_al1 = sem("s_al1"); s_al2 = sem("s_al2")
        s_b1l = sem("s_b1l"); s_a2w = sem("s_a2w"); s_a2l = sem("s_a2l")
        s_wqk = sem("s_wqk")
        s_trB = [sem(f"s_trB{k}") for k in range(2)]
        s_xc = [sem(f"s_xc{k}") for k in range(4)]
        s_teK = sem("s_teK"); s_kev = sem("s_kev"); s_teMM = sem("s_teMM")
        s_teQ = sem("s_teQ"); s_qev = sem("s_qev"); s_mev = sem("s_mev")
        s_jr = [sem(f"s_jr{k}") for k in range(6)]
        s_cb = sem("s_cb")
        cbn = [0]
        s_xr = sem("s_xr"); s_vc1 = sem("s_vc1"); s_sq2 = sem("s_sq2")
        s_vrec2 = sem("s_vrec2"); s_rs2 = sem("s_rs2"); s_vnm2 = sem("s_vnm2")
        s_z = sem("s_z"); s_teC = sem("s_teC"); s_vq = sem("s_vq")
        s_gbc = sem("s_gbc"); s_cbc = sem("s_cbc"); s_abc = sem("s_abc")
        s_trD = [sem(f"s_trD{k}") for k in range(2)]
        s_mm1 = sem("s_mm1"); s_ge = sem("s_ge"); s_mm2 = sem("s_mm2")
        s_epu = sem("s_epu"); s_epv = sem("s_epv"); s_vw = sem("s_vw")
        s_vo = sem("s_vo")
        s_ow = [sem(f"s_ow{k}") for k in range(2)]


        wg_slot = {3 + 2 * g: g for g in range(16)}
        wn_slot = {4 + 2 * u: u for u in range(16)}

        with ExitStack() as sA:
            Jring = sA.enter_context(nc.sbuf_tensor("Jring", [P, 10 * D], BF16))
            XNr = sA.enter_context(nc.sbuf_tensor("XNr", [P, 3 * D], BF16))
            wTbf = sA.enter_context(nc.sbuf_tensor("wTbf", [P, 2 * 4096], BF16))
            sgnb = sA.enter_context(nc.sbuf_tensor("sgnb", [P, 4096], BF16))
            wnb = sA.enter_context(nc.sbuf_tensor("wnb", [P, 2 * 4096], BF16))
            XT = sA.enter_context(nc.sbuf_tensor("XT", [P, 2 * 8192], BF16))
            XNc = sA.enter_context(nc.sbuf_tensor("XNc", [P, 4 * D], BF16))
            WQTb = sA.enter_context(nc.sbuf_tensor("WQTb", [P, D], BF16))
            WKTb = sA.enter_context(nc.sbuf_tensor("WKTb", [P, D], BF16))
            Kb = sA.enter_context(nc.sbuf_tensor("Kb", [P, 2 * 128], BF16))
            pK = [sA.enter_context(nc.psum_tensor(f"pK{x}", [P, 128], F32)) for x in range(2)]
            pQT = [sA.enter_context(nc.psum_tensor(f"pQT{x}", [P, 256], F32)) for x in range(2)]
            pM = sA.enter_context(nc.psum_tensor("pM", [P, D], F32))

            with nc.Block() as blk:
                @blk.gpsimd
                def _(gpsimd):
                    def wt_load(g):
                        base = (g % 2) * 4096
                        if g < 8:
                            gpsimd.dma_start(wTbf[:, base:base + 4096],
                                             dw1T[g * 128:(g + 1) * 128, :]
                                             ).then_inc(s_wt[g % 2], 16)
                        else:
                            t = g - 8
                            gpsimd.dma_start(
                                wTbf[:, base:base + 4096]
                                .rearrange("p (a b) -> p a b", b=D),
                                bass.AP(dw2T, t * 512 * D,
                                        [[D, 128], [128 * D, 4], [1, D]])
                                ).then_inc(s_wt[g % 2], 16)

                    def wn_load(u):
                        base = (u % 2) * 4096
                        if u < 8:
                            gpsimd.dma_start(
                                wnb[:, base:base + 4096]
                                .rearrange("p (a b) -> p a b", b=D),
                                bass.AP(dw1n, u * 512 * D,
                                        [[D, 128], [128 * D, 4], [1, D]])
                                ).then_inc(s_wn[u % 2], 16)
                        else:
                            t = u - 8
                            gpsimd.dma_start(wnb[:, base:base + 4096],
                                             dw2n[t * 128:(t + 1) * 128, :]
                                             ).then_inc(s_wn[u % 2], 16)

                    wt_load(0)
                    wt_load(1)
                    wn_load(0)
                    wn_load(1)
                    for c in range(4):
                        gpsimd.dma_start(Jring[:, (c % 10) * D:(c % 10 + 1) * D],
                                         dJ[c * 128:(c + 1) * 128, :]).then_inc(s_j[c % 10], 16)
                    for dc in range(DC):
                        gpsimd.dma_start(WQTb[:, dc * 128:(dc + 1) * 128],
                                         dwqT[dc * 128:(dc + 1) * 128, :]).then_inc(s_wqk, 16)
                    for dc in range(DC):
                        gpsimd.dma_start(WKTb[:, dc * 128:(dc + 1) * 128],
                                         dwkT[dc * 128:(dc + 1) * 128, :]).then_inc(s_wqk, 16)
                    for c in range(41):
                        if 4 <= c + 2 < NCH:
                            if c - 7 >= 1:
                                gpsimd.wait_ge(s_xn, c - 7)
                            gpsimd.dma_start(Jring[:, ((c + 2) % 10) * D:((c + 2) % 10 + 1) * D],
                                             dJ[(c + 2) * 128:(c + 3) * 128, :]
                                             ).then_inc(s_j[(c + 2) % 10], 16)
                        cx = c - 14
                        if 0 <= cx <= 25:
                            gpsimd.wait_ge(s_xw[cx % 3], 16 * (cx // 3 + 1))
                            if cx >= 4:
                                gpsimd.wait_ge(s_teMM, cx - 3)
                            gpsimd.dma_start(XNc[:, (cx % 4) * D:(cx % 4 + 1) * D],
                                             dxnat[cx * 128:(cx + 1) * 128, :]
                                             ).then_inc(s_xc[cx % 4], 16)
                        if c in wg_slot and wg_slot[c] + 2 < 16:
                            g = wg_slot[c] + 2
                            gpsimd.wait_ge(s_sg, g - 1)
                            wt_load(g)
                        if c in wn_slot and wn_slot[c] + 2 < 16:
                            u = wn_slot[c] + 2
                            gpsimd.wait_ge(s_ab, 4 * (u - 1))
                            wn_load(u)
                    for cx in range(26, NCH):
                        gpsimd.wait_ge(s_xw[cx % 3], 16 * (cx // 3 + 1))
                        gpsimd.wait_ge(s_teMM, cx - 3)
                        gpsimd.dma_start(XNc[:, (cx % 4) * D:(cx % 4 + 1) * D],
                                         dxnat[cx * 128:(cx + 1) * 128, :]
                                         ).then_inc(s_xc[cx % 4], 16)

                @blk.sync
                def _(sync):
                    sync.dma_start(b1P[:], db1P.ap()).then_inc(s_b1l, 16)
                    for c in range(41):
                        if 0 <= c - 8 < NCH:
                            cc = c - 8
                            sync.wait_ge(s_xn, cc + 1)
                            sync.dma_start(dxnat[cc * 128:(cc + 1) * 128, :],
                                           XNr[:, (cc % 3) * D:(cc % 3 + 1) * D]
                                           ).then_inc(s_xw[cc % 3], 16)
                        if c in (16, 24, 32, 40):
                            q = {16: 0, 24: 1, 32: 2, 40: 3}[c]
                            if q >= 2:
                                sync.wait_ge(s_teQ, 4 * (q - 2) + 4)
                            for k in range(3):
                                ndone = 8 * (q + 1)
                                cnt = (ndone - 1 - k) // 3 + 1 if ndone - 1 >= k else 0
                                if cnt > 0:
                                    sync.wait_ge(s_xw[k], 16 * cnt)
                            for dc in range(DC):
                                sync.dma_start(
                                    XT[:, (q % 2) * 8192 + dc * 1024:
                                          (q % 2) * 8192 + dc * 1024 + 1024],
                                    bass.AP(dxnat, q * 1024 * D + dc * 128,
                                            [[D, 1024], [1, 128]]),
                                    transpose=True).then_inc(s_trB[q % 2], 16)
                            if q < 2:
                                for dc in range(DC):
                                    sync.dma_start(
                                        XTD[:, (q % 2) * 2048 + dc * 256:
                                               (q % 2) * 2048 + dc * 256 + 256],
                                        bass.AP(dxnat, q * 256 * D + dc * 128,
                                                [[D, 256], [1, 128]]),
                                        transpose=True).then_inc(s_trD[q % 2], 16)

                @blk.vector
                def _(vector):
                    vector.memset(onesF[:], 1.0)
                    for c in range(41):
                        if c < NCH:
                            vector.wait_ge(s_j[c % 10], 16 * (c // 10 + 1))
                            vector.reduce_sum(SU[:, c:c + 1],
                                              Jring[:, (c % 10) * D:(c % 10 + 1) * D],
                                              axis=AX)
                        if c in wg_slot:
                            g = wg_slot[c]
                            vector.wait_ge(s_wt[g % 2], 16 * (g // 2 + 1))
                            vector.tensor_scalar(sgnb[:], wTbf[:, (g % 2) * 4096:
                                                               (g % 2) * 4096 + 4096],
                                                 1e38, 1.0, ALU.mult, ALU.min)
                        if 0 <= c - 2 < NCH:
                            vector.wait_ge(s_sq, c - 1)
                            vector.tensor_scalar(E2E[:, c - 2:c - 1], SQ[:, c - 2:c - 1],
                                                 INV_D, EPS, ALU.mult, ALU.add)
                            vector.tensor_scalar(MU[:, c - 2:c - 1], SU[:, c - 2:c - 1],
                                                 INV_D, None, ALU.mult)
                            vector.tensor_scalar(NEGMU[:, c - 2:c - 1], SU[:, c - 2:c - 1],
                                                 -INV_D, None, ALU.mult)
                        elif c in wg_slot:
                            vector.memset(junk1[:, jctr[0] % 8:jctr[0] % 8 + 1], 0.0)
                            jctr[0] += 1
                            vector.memset(junk1[:, jctr[0] % 8:jctr[0] % 8 + 1], 0.0)
                            jctr[0] += 1
                        if c in wg_slot:
                            g = wg_slot[c]
                            if g < 8:
                                dst = S1T[:, g * H: g * H + 4096]
                            else:
                                dst = S2T[:, (g - 8) * 4096: (g - 8) * 4096 + 4096]
                            vector.tensor_scalar(dst, sgnb[:], -1.0, None,
                                                 ALU.max).then_inc(s_sg, 1)
                        if 0 <= c - 3 < NCH:
                            vector.scalar_tensor_tensor(VAR[:, c - 3:c - 2],
                                                        MU[:, c - 3:c - 2],
                                                        NEGMU[:, c - 3:c - 2],
                                                        E2E[:, c - 3:c - 2],
                                                        ALU.mult, ALU.add)
                        if 0 <= c - 4 < NCH:
                            vector.reciprocal(REC[:, c - 4:c - 3],
                                              VAR[:, c - 4:c - 3]).then_inc(s_vrec, 1)
                        if 0 <= c - 6 < NCH:
                            vector.wait_ge(s_rs, c - 5)
                            vector.tensor_tensor(NMS[:, c - 6:c - 5],
                                                 NEGMU[:, c - 6:c - 5],
                                                 RS[:, c - 6:c - 5],
                                                 ALU.mult).then_inc(s_vnm, 1)
                        if 0 <= c - 7 < NCH:
                            cc = c - 7
                            if cc >= 3:
                                vector.wait_ge(s_xw[cc % 3], 16 * (cc // 3))
                            vector.tensor_scalar(XNr[:, (cc % 3) * D:(cc % 3 + 1) * D],
                                                 Jring[:, (cc % 10) * D:(cc % 10 + 1) * D],
                                                 RS[:, cc:cc + 1], NMS[:, cc:cc + 1],
                                                 ALU.mult, ALU.add).then_inc(s_xn, 1)
                        if c == 21:
                            vector.wait_ge(s_ab, 32)
                            vector.tensor_scalar(alpha1P[:], AL1[:], INV_D, None,
                                                 ALU.mult).then_inc(s_al1, 1)
                        if 17 <= c <= 40:
                            idx = c - 17
                            nb, half = idx // 2, idx % 2
                            cch = 2 * nb + half
                            vector.wait_ge(s_teK, cch + 1)
                            vector.tensor_copy(Kb[:, (cch % 2) * 128:(cch % 2 + 1) * 128],
                                               pK[cch % 2][:]).then_inc(s_kev, 1)
                            if half == 1:
                                vector.wait_ge(s_teQ, nb + 1)
                                vector.tensor_copy(QT[:, nb * 256:(nb + 1) * 256],
                                                   pQT[nb % 2][:]).then_inc(s_qev, 1)
                    tail_alpha = []
                    tail_alpha.append(("al1",))
                    tail_alpha.append(("q1",))
                    tail_alpha.append(("q2",))
                    tail_alpha.append(("r",))
                    tail_alpha.append(("s",))

                    def emit_alpha(tag):
                        if tag == "al1":
                            vector.wait_ge(s_ab, 32)
                            vector.tensor_scalar(alpha1P[:], AL1[:], INV_D, None,
                                                 ALU.mult).then_inc(s_al1, 1)
                        elif tag == "q1":
                            vector.wait_ge(s_ab, 64)
                            vector.tensor_tensor(AL2q[:, 0:8], AL2p[:, 0:8],
                                                 AL2p[:, 8:16], ALU.add)
                        elif tag == "q2":
                            vector.tensor_tensor(AL2q[:, 8:16], AL2p[:, 16:24],
                                                 AL2p[:, 24:32], ALU.add)
                        elif tag == "r":
                            vector.tensor_tensor(AL2r[:], AL2q[:, 0:8],
                                                 AL2q[:, 8:16], ALU.add)
                        elif tag == "s":
                            vector.tensor_scalar(AL2s[:], AL2r[:], INV_H, None,
                                                 ALU.mult).then_inc(s_al2, 1)

                    ai = 1
                    for nb in range(12, NBLK):
                        for half in range(2):
                            cch = 2 * nb + half
                            vector.wait_ge(s_teK, cch + 1)
                            vector.tensor_copy(Kb[:, (cch % 2) * 128:(cch % 2 + 1) * 128],
                                               pK[cch % 2][:]).then_inc(s_kev, 1)
                            if ai < len(tail_alpha):
                                emit_alpha(tail_alpha[ai][0])
                                ai += 1
                        vector.wait_ge(s_teQ, nb + 1)
                        vector.tensor_copy(QT[:, nb * 256:(nb + 1) * 256],
                                           pQT[nb % 2][:]).then_inc(s_qev, 1)
                    while ai < len(tail_alpha):
                        vector.memset(junk1[:, jctr[0] % 8:jctr[0] % 8 + 1], 0.0)
                        jctr[0] += 1
                        emit_alpha(tail_alpha[ai][0])
                        ai += 1
                    vector.wait_ge(s_teMM, NCH)
                    vector.tensor_copy(Msb[:], pM[:]).then_inc(s_mev, 1)

                @blk.scalar
                def _(scalar):
                    for c in range(41):
                        if 0 <= c - 5 < NCH:
                            scalar.wait_ge(s_vrec, c - 4)
                            scalar.sqrt(RS[:, c - 5:c - 4],
                                        REC[:, c - 5:c - 4]).then_inc(s_rs, 1)
                        if c < NCH:
                            scalar.wait_ge(s_j[c % 10], 16 * (c // 10 + 1))
                            scalar.activation(dumpA[:],
                                              Jring[:, (c % 10) * D:(c % 10 + 1) * D],
                                              ACT.Square,
                                              accum_out=SQ[:, c:c + 1]).then_inc(s_sq, 1)
                        if c in wn_slot:
                            u = wn_slot[c]
                            scalar.wait_ge(s_wn[u % 2], 16 * (u // 2 + 1))
                            base = (u % 2) * 4096
                            for k in range(4):
                                if u < 8:
                                    t = 4 * u + k
                                    tgt = AL1[:, (t % 4) * 8 + t // 4:
                                                 (t % 4) * 8 + t // 4 + 1]
                                else:
                                    tt = u - 8
                                    tgt = AL2p[:, k * 8 + tt: k * 8 + tt + 1]
                                scalar.activation(dumpA[:],
                                                  wnb[:, base + k * D:base + (k + 1) * D],
                                                  ACT.Abs, accum_out=tgt).then_inc(s_ab, 1)

                @blk.tensor
                def _(tensor):
                    tensor.wait_ge(s_wqk, 16 * 16)
                    for q in range(4):
                        xt0 = (q % 2) * 8192
                        tensor.wait_ge(s_trB[q % 2], 16 * 8 * (q // 2 + 1))
                        for nb in range(4 * q, 4 * q + 4):
                            off = (nb % 4) * 256
                            for j in range(2):
                                cch = 2 * nb + j
                                if cch >= 2:
                                    tensor.wait_ge(s_kev, cch - 1)
                                for dc in range(DC):
                                    mm = tensor.matmul(
                                        pK[cch % 2][:],
                                        XT[:, xt0 + dc * 1024 + off + j * 128:
                                              xt0 + dc * 1024 + off + j * 128 + 128],
                                        WKTb[:, dc * 128:(dc + 1) * 128],
                                        start=(dc == 0), stop=(dc == 7),
                                        skip_group_check=True)
                                mm.then_inc(s_teK, 1)
                            for j in range(2):
                                cch = 2 * nb + j
                                tensor.wait_ge(s_kev, cch + 1)
                                tensor.wait_ge(s_xc[cch % 4], 16 * (cch // 4 + 1))
                                for hf in range(2):
                                    mm = tensor.matmul(
                                        pM[:, hf * 512:(hf + 1) * 512],
                                        Kb[:, (cch % 2) * 128:(cch % 2 + 1) * 128],
                                        XNc[:, (cch % 4) * D + hf * 512:
                                               (cch % 4) * D + hf * 512 + 512],
                                        start=(cch == 0), stop=(cch == NCH - 1),
                                        skip_group_check=True)
                                mm.then_inc(s_teMM, 1)
                            if nb >= 2:
                                tensor.wait_ge(s_qev, nb - 1)
                            for dc in range(DC):
                                mm = tensor.matmul(
                                    pQT[nb % 2][:],
                                    WQTb[:, dc * 128:(dc + 1) * 128],
                                    XT[:, xt0 + dc * 1024 + off: xt0 + dc * 1024 + off + 256],
                                    start=(dc == 0), stop=(dc == 7),
                                    skip_group_check=True)
                            mm.then_inc(s_teQ, 1)

        with ExitStack() as sD:
            G = sD.enter_context(nc.sbuf_tensor("G", [P, HT * NB], BF16))
            a2BC = sD.enter_context(nc.sbuf_tensor("a2BC", [P, D], F32))
            CBC = sD.enter_context(nc.sbuf_tensor("CBC", [P, D], F32))
            a2row = sD.enter_context(nc.sbuf_tensor("a2row", [1, D], F32))
            gBC = sD.enter_context(nc.sbuf_tensor("gBC", [P, D], F32))
            Jr = sD.enter_context(nc.sbuf_tensor("Jr", [P, 6 * D], F32))
            xrb = sD.enter_context(nc.sbuf_tensor("xrb", [P, 2 * D], BF16))
            tbuf = sD.enter_context(nc.sbuf_tensor("tbuf", [P, 4 * D], BF16))
            zbuf = sD.enter_context(nc.sbuf_tensor("zbuf", [P, 2 * D], BF16))
            qbuf = sD.enter_context(nc.sbuf_tensor("qbuf", [P, 4 * D], BF16))
            ubuf = sD.enter_context(nc.sbuf_tensor("ubuf", [P, 2 * D], F32))
            vbuf = sD.enter_context(nc.sbuf_tensor("vbuf", [P, D], F32))
            wbuf = sD.enter_context(nc.sbuf_tensor("wbuf", [P, 2 * D], F32))
            ostg = sD.enter_context(nc.sbuf_tensor("ostg", [P, 2 * D], F32))
            pg = [sD.enter_context(nc.psum_tensor(f"pg{x}", [P, NB], F32)) for x in range(4)]
            po = [sD.enter_context(nc.psum_tensor(f"po{x}", [P, 512], F32)) for x in range(2)]
            pA = sD.enter_context(nc.psum_tensor("pA", [P, D], F32))
            with nc.Block() as blk:
                @blk.sync
                def _(sync):
                    sync.dma_start(gBC[:], dgB.ap()).then_inc(s_gbc, 16)
                    sync.dma_start(ubuf[:, 0:D], dbB.ap()).then_inc(s_cbc, 16)
                    sync.dma_start(wbuf[:, 0:D], db2B.ap()).then_inc(s_cbc, 16)
                    for i in range(6):
                        sync.dma_start(Jr[:, (i % 6) * D:(i % 6 + 1) * D],
                                       dJ[i * 128:(i + 1) * 128, :]).then_inc(s_jr[i % 6], 16)
                    sync.wait_ge(s_al2, 1)
                    sync.dma_start(bass.AP(da2, 0, [[1, 128], [128, 8]]), AL2s[:],
                                   allow_slow_non_contiguous=True).then_inc(s_a2w, 16)
                    sync.wait_ge(s_a2w, 16)
                    sync.dma_start(a2row[:], bass.AP(da2, 0, [[1, 1], [1, D]])
                                   ).then_inc(s_a2l, 16)
                    for nb in range(NBLK):
                        if nb + 2 < NBLK:
                            nbp = nb + 2
                            sync.wait_ge(s_mm1, HT * (nbp - 1))
                            for dc in range(DC):
                                sync.dma_start(
                                    XTD[:, (nbp % 2) * 2048 + dc * 256:
                                           (nbp % 2) * 2048 + dc * 256 + 256],
                                    bass.AP(dxnat, nbp * 256 * D + dc * 128,
                                            [[D, 256], [1, 128]]),
                                    transpose=True).then_inc(s_trD[nbp % 2], 16)
                        for st in range(NSUB):
                            i = nb * NSUB + st
                            if i + 6 < NT:
                                sync.wait_ge(s_vw, i + 1)
                                sync.dma_start(Jr[:, ((i + 6) % 6) * D:((i + 6) % 6 + 1) * D],
                                               dJ[(i + 6) * 128:(i + 7) * 128, :]
                                               ).then_inc(s_jr[(i + 6) % 6], 16)
                            if i - 3 >= 0:
                                ii = i - 3
                                sync.wait_ge(s_vo, ii + 1)
                                sync.dma_start(dout[ii * 128:(ii + 1) * 128, :],
                                               ostg[:, (ii % 2) * D:(ii % 2 + 1) * D]
                                               ).then_inc(s_ow[ii % 2], 16)
                    for ii in range(NT - 3, NT):
                        sync.wait_ge(s_vo, ii + 1)
                        sync.dma_start(dout[ii * 128:(ii + 1) * 128, :],
                                       ostg[:, (ii % 2) * D:(ii % 2 + 1) * D]
                                       ).then_inc(s_ow[ii % 2], 16)
                    sync.wait_ge(s_ow[0], 16 * (NT // 2))
                    sync.wait_ge(s_ow[1], 16 * (NT // 2))

                @blk.vector
                def _(vector):
                    def vb(ins):
                        ins.then_inc(s_cb, 1)
                        cbn[0] += 1
                        vector.wait_ge(s_cb, cbn[0])

                    def chain(i):
                        vector.wait_ge(s_jr[i % 6], 16 * (i // 6 + 1))
                        vb(vector.tensor_scalar(xrb[:, (i % 2) * D:(i % 2 + 1) * D],
                                                Jr[:, (i % 6) * D:(i % 6 + 1) * D],
                                                RS[:, i:i + 1], NMS[:, i:i + 1],
                                                ALU.mult, ALU.add))
                        vector.wait_ge(s_teC, i + 1)
                        vector.scalar_tensor_tensor(
                            tbuf[:, (i % 4) * D:(i % 4 + 1) * D],
                            pA[:], ISQ,
                            xrb[:, (i % 2) * D:(i % 2 + 1) * D],
                            ALU.mult, ALU.add,
                            accum_out=SU[:, i:i + 1]).then_inc(s_vc1, 1)
                        vector.wait_ge(s_vc1, i + 1)
                        vb(vector.scalar_tensor_tensor(
                            dumpA[:],
                            tbuf[:, (i % 4) * D:(i % 4 + 1) * D], 1.0,
                            tbuf[:, (i % 4) * D:(i % 4 + 1) * D],
                            ALU.mult, ALU.mult,
                            accum_out=SQ[:, i:i + 1]))
                        vector.tensor_scalar(E2E[:, i:i + 1], SQ[:, i:i + 1],
                                             INV_D, EPS, ALU.mult, ALU.add)
                        vector.tensor_scalar(MU[:, i:i + 1], SU[:, i:i + 1],
                                             INV_D, None, ALU.mult)
                        vb(vector.tensor_scalar(NEGMU[:, i:i + 1], SU[:, i:i + 1],
                                                -INV_D, None, ALU.mult))
                        vb(vector.scalar_tensor_tensor(VAR[:, i:i + 1],
                                                       MU[:, i:i + 1],
                                                       NEGMU[:, i:i + 1],
                                                       E2E[:, i:i + 1],
                                                       ALU.mult, ALU.add))
                        vi = VAR[:, i:i + 1].bitcast(mybir.dt.int32)
                        vb(vector.tensor_scalar(NTI1[:, i:i + 1], vi, 1, None,
                                                ALU.arith_shift_right))
                        vb(vector.tensor_scalar(NTI2[:, i:i + 1], NTI1[:, i:i + 1],
                                                -1, None, ALU.mult))
                        vb(vector.tensor_scalar(NTIY[:, i:i + 1], NTI2[:, i:i + 1],
                                                0x5f3759df, None, ALU.add))
                        ys = [NTIY[:, i:i + 1].bitcast(F32),
                              NTYa[:, i:i + 1], NTYb[:, i:i + 1]]
                        cur = ys[0]
                        for k in range(3):
                            nxt = ys[1 + (k % 2)] if k < 2 else NTYa[:, i:i + 1]
                            if k == 1:
                                nxt = NTYb[:, i:i + 1]
                            if k == 2:
                                nxt = NTYa[:, i:i + 1]
                            vb(vector.tensor_tensor(NTA[:, i:i + 1], cur, cur, ALU.mult))
                            vb(vector.tensor_tensor(NTB[:, i:i + 1], NTA[:, i:i + 1],
                                                    VAR[:, i:i + 1], ALU.mult))
                            vb(vector.tensor_scalar(NTH[:, i:i + 1], NTB[:, i:i + 1],
                                                    -0.5, 1.5, ALU.mult, ALU.add))
                            vb(vector.tensor_tensor(nxt, cur, NTH[:, i:i + 1], ALU.mult))
                            cur = nxt
                        vector.tensor_tensor(NMS2[:, i:i + 1],
                                             NEGMU[:, i:i + 1], cur,
                                             ALU.mult).then_inc(s_vnm2, 1)
                        vector.wait_ge(s_vnm2, i + 1)
                        if i >= 2:
                            vector.wait_ge(s_vq, i - 1)
                        vb(vector.tensor_scalar(zbuf[:, (i % 2) * D:(i % 2 + 1) * D],
                                                tbuf[:, (i % 4) * D:(i % 4 + 1) * D],
                                                NTYa[:, i:i + 1], NMS2[:, i:i + 1],
                                                ALU.mult, ALU.add))
                        vector.wait_ge(s_gbc, 16)
                        vector.tensor_tensor(qbuf[:, (i % 4) * D:(i % 4 + 1) * D],
                                             zbuf[:, (i % 2) * D:(i % 2 + 1) * D],
                                             gBC[:], ALU.mult).then_inc(s_vq, 1)

                    vector.wait_ge(s_abc, 1)
                    vector.tensor_copy(a2BC[:], pA[:]).then_inc(s_abc, 1)
                    vector.wait_ge(s_cbc, 32)
                    vector.tensor_tensor(CBC[:], ubuf[:, 0:D], wbuf[:, 0:D],
                                         ALU.add).then_inc(s_cbc, 1)
                    chain(0)
                    chain(1)
                    for slot in range(NT + 3):
                        if slot + 2 < NT:
                            chain(slot + 2)
                        i = slot
                        if i < NT:
                            for hf in range(2):
                                hidx = 2 * i + hf
                                vector.wait_ge(s_mm2, hidx + 1)
                                vector.tensor_tensor(
                                    ubuf[:, (i % 2) * D + hf * 512:
                                            (i % 2) * D + hf * 512 + 512],
                                    po[hidx % 2][:],
                                    a2BC[:, hf * 512:(hf + 1) * 512],
                                    ALU.mult).then_inc(s_epu, 1)
                        i2 = slot - 2
                        if 0 <= i2 < NT:
                            vector.wait_ge(s_jr[i2 % 6], 16 * (i2 // 6 + 1))
                            vector.tensor_tensor(wbuf[:, (i2 % 2) * D:(i2 % 2 + 1) * D],
                                                 vbuf[:],
                                                 Jr[:, (i2 % 6) * D:(i2 % 6 + 1) * D],
                                                 ALU.add).then_inc(s_vw, 1)
                        i1 = slot - 1
                        if 0 <= i1 < NT:
                            vector.wait_ge(s_vq, i1 + 1)
                            vector.tensor_tensor(vbuf[:],
                                                 ubuf[:, (i1 % 2) * D:(i1 % 2 + 1) * D],
                                                 qbuf[:, (i1 % 4) * D:(i1 % 4 + 1) * D],
                                                 ALU.add).then_inc(s_epv, 1)
                        i3 = slot - 3
                        if 0 <= i3 < NT:
                            if i3 >= 2:
                                vector.wait_ge(s_ow[i3 % 2], 16 * (i3 // 2))
                            vector.tensor_tensor(ostg[:, (i3 % 2) * D:(i3 % 2 + 1) * D],
                                                 wbuf[:, (i3 % 2) * D:(i3 % 2 + 1) * D],
                                                 CBC[:], ALU.add).then_inc(s_vo, 1)

                @blk.scalar
                def _(scalar):
                    scalar.wait_ge(s_al1, 1)
                    scalar.wait_ge(s_b1l, 16)
                    for nb in range(NBLK):
                        for ht in range(HT):
                            g_idx = nb * HT + ht
                            scalar.wait_ge(s_mm1, g_idx + 1)
                            a1c = (ht % 4) * 8 + ht // 4
                            scalar.activation(G[:, ht * NB:(ht + 1) * NB],
                                              pg[g_idx % 4][:], ACT.Gelu,
                                              bias=b1P[:, ht:ht + 1],
                                              scale=alpha1P[:, a1c:a1c + 1]).then_inc(s_ge, 1)

                @blk.tensor
                def _(tensor):
                    def pa_mm(i):
                        if i >= 1:
                            tensor.wait_ge(s_vc1, i)
                        for hf in range(2):
                            mm = tensor.matmul(
                                pA[:, hf * 512:(hf + 1) * 512],
                                QT[:, i * 128:(i + 1) * 128],
                                Msb[:, hf * 512:(hf + 1) * 512],
                                start=True, stop=True, skip_group_check=True)
                        mm.then_inc(s_teC, 1)

                    tensor.wait_ge(s_sg, 16)
                    for nb in range(NBLK):
                        xt0 = (nb % 2) * 2048
                        tensor.wait_ge(s_trD[nb % 2], 16 * 8 * (nb // 2 + 1))
                        for ht in range(HT):
                            g_idx = nb * HT + ht
                            if nb == 0 and ht == 8:
                                tensor.wait_ge(s_a2l, 16)
                                for hf in range(2):
                                    mm = tensor.matmul(
                                        pA[:, hf * 512:(hf + 1) * 512],
                                        onesF[:], a2row[:1, hf * 512:(hf + 1) * 512],
                                        start=True, stop=True, skip_group_check=True)
                                mm.then_inc(s_abc, 1)
                            if nb == 0 and ht == 12:
                                tensor.wait_ge(s_abc, 2)
                                tensor.wait_ge(s_mev, 1)
                                pa_mm(0)
                            if ht == 2 and nb >= 1 and 2 * nb + 1 < NT:
                                pa_mm(2 * nb + 1)
                            if ht == 16 and nb >= 1 and 2 * nb + 2 < NT:
                                pa_mm(2 * nb + 2)
                            if nb == 0 and ht == 20:
                                pa_mm(1)
                            if nb == 0 and ht == 24:
                                pa_mm(2)
                            if g_idx >= 4:
                                tensor.wait_ge(s_ge, g_idx - 3)
                            for dc in range(DC):
                                mm = tensor.matmul(
                                    pg[g_idx % 4][:],
                                    S1T[:, dc * H + ht * 128: dc * H + ht * 128 + 128],
                                    XTD[:, xt0 + dc * 256: xt0 + dc * 256 + 256],
                                    start=(dc == 0), stop=(dc == 7),
                                    skip_group_check=True)
                            mm.then_inc(s_mm1, 1)
                        tensor.wait_ge(s_ge, HT * (nb + 1))
                        for st in range(NSUB):
                            i = nb * NSUB + st
                            for hf in range(2):
                                hidx = 2 * i + hf
                                if hidx >= 2:
                                    tensor.wait_ge(s_epu, hidx - 1)
                                for hc in range(HT):
                                    mm = tensor.matmul(
                                        po[hidx % 2][:],
                                        G[:, hc * NB + st * 128: hc * NB + st * 128 + 128],
                                        S2T[:, hc * D + hf * 512: hc * D + hf * 512 + 512],
                                        start=(hc == 0), stop=(hc == HT - 1),
                                        skip_group_check=True)
                                mm.then_inc(s_mm2, 1)

    return nc


_NC_CACHE = None


def kernel(L0_attn_J, L0_attn_W_Q, L0_attn_W_K, L0_attn_gamma, L0_attn_beta,
           L0_ffn_w1, L0_ffn_b1, L0_ffn_w2, L0_ffn_b2):
    J = np.ascontiguousarray(np.asarray(L0_attn_J, dtype=np.float32))
    wq = np.asarray(L0_attn_W_Q, dtype=np.float32)
    wk = np.asarray(L0_attn_W_K, dtype=np.float32)
    gamma = np.asarray(L0_attn_gamma, dtype=np.float32)
    beta = np.asarray(L0_attn_beta, dtype=np.float32)
    w1 = np.ascontiguousarray(np.asarray(L0_ffn_w1, dtype=np.float32))
    b1 = np.asarray(L0_ffn_b1, dtype=np.float32)
    w2 = np.ascontiguousarray(np.asarray(L0_ffn_w2, dtype=np.float32))
    b2 = np.asarray(L0_ffn_b2, dtype=np.float32)

    wqT = np.ascontiguousarray(wq.T)
    wkT = np.ascontiguousarray(wk.T)
    w1T = np.ascontiguousarray(w1.T)
    w2T = np.ascontiguousarray(w2.T)
    gB = np.ascontiguousarray(np.broadcast_to(gamma[None, :], (P, D)))
    bB = np.ascontiguousarray(np.broadcast_to(beta[None, :], (P, D)))
    b2B = np.ascontiguousarray(np.broadcast_to(b2[None, :], (P, D)))
    b1P = np.ascontiguousarray(b1.reshape(32, 128).T)

    global _NC_CACHE
    if _NC_CACHE is None:
        _NC_CACHE = build_nc()
    nc = _NC_CACHE

    in_maps = []
    for b in range(8):
        in_maps.append({
            "J": np.ascontiguousarray(J[b]),
            "wqT": wqT, "wkT": wkT, "w1T": w1T, "w2T": w2T,
            "w1n": w1, "w2n": w2,
            "gB": gB, "bB": bB, "b2B": b2B, "b1P": b1P,
        })
    res = run_bass_kernel_spmd(nc, in_maps, core_ids=list(range(8)))
    out = np.stack([np.asarray(r["out"], dtype=np.float32) for r in res.results], axis=0)
    return out


# version 11
# speedup vs baseline: 1.0401x; 1.0401x over previous
import math
import numpy as np
from contextlib import ExitStack

import concourse.bass as bass
import concourse.mybir as mybir
from concourse.bass_utils import run_bass_kernel_spmd

F32 = mybir.dt.float32
BF16 = mybir.dt.bfloat16
FP8 = mybir.dt.float8e4
ALU = mybir.AluOpType
ACT = mybir.ActivationFunctionType
AX = mybir.AxisListType.X

P = 128
D = 1024
H = 4096
N = 4096
NCH = 32
DC = 8
HT = 32
NB = 256
NBLK = 16
NSUB = 2
NT = NBLK * NSUB
INV_D = 1.0 / 1024.0
INV_H = 1.0 / 4096.0
EPS = 1e-5
ISQ = 1.0 / math.sqrt(128.0)


def build_nc():
    nc = bass.Bass("TRN2", target_bir_lowering=False, detect_race_conditions=False)

    dJ = nc.dram_tensor("J", [N, D], F32, kind="ExternalInput")
    dwqT = nc.dram_tensor("wqT", [D, 128], F32, kind="ExternalInput")
    dwkT = nc.dram_tensor("wkT", [D, 128], F32, kind="ExternalInput")
    dw1T = nc.dram_tensor("w1T", [D, H], F32, kind="ExternalInput")
    dw2T = nc.dram_tensor("w2T", [H, D], F32, kind="ExternalInput")
    dw1n = nc.dram_tensor("w1n", [H, D], F32, kind="ExternalInput")
    dw2n = nc.dram_tensor("w2n", [D, H], F32, kind="ExternalInput")
    dgB = nc.dram_tensor("gB", [P, D], F32, kind="ExternalInput")
    dbB = nc.dram_tensor("bB", [P, D], F32, kind="ExternalInput")
    db2B = nc.dram_tensor("b2B", [P, D], F32, kind="ExternalInput")
    db1P = nc.dram_tensor("b1P", [P, 32], F32, kind="ExternalInput")
    diden = nc.dram_tensor("iden", [P, P], F32, kind="ExternalInput")
    dout = nc.dram_tensor("out", [N, D], F32, kind="ExternalOutput")
    dxnat = nc.dram_tensor("xnat", [N, D], BF16, kind="Internal")
    da2 = nc.dram_tensor("a2scr", [D], F32, kind="Internal")

    ctx = ExitStack()
    sb = lambda n, s, d: ctx.enter_context(nc.sbuf_tensor(n, s, d))
    sem = lambda n: ctx.enter_context(nc.semaphore(n))

    with ctx:
        S1T = sb("S1T", [P, DC * H], FP8)
        S2T = sb("S2T", [P, HT * D], FP8)
        QT = sb("QT", [P, N], BF16)
        Msb = sb("Msb", [P, D], BF16)
        alpha1P = sb("alpha1P", [P, 32], F32)
        AL1 = sb("AL1", [P, 32], F32)
        AL2p = sb("AL2p", [P, 32], F32)
        AL2q = sb("AL2q", [P, 16], F32)
        AL2r = sb("AL2r", [P, 8], F32)
        AL2s = sb("AL2s", [P, 8], F32)
        b1P = sb("b1Ps", [P, 32], F32)
        onesF = sb("onesF", [1, P], F32)
        junk1 = sb("junk1", [P, 8], F32)
        jctr = [0]
        SU = sb("SU", [P, 32], F32)
        SQ = sb("SQ", [P, 32], F32)
        MU = sb("MU", [P, 32], F32)
        E2E = sb("E2E", [P, 32], F32)
        NEGMU = sb("NEGMU", [P, 32], F32)
        VAR = sb("VAR", [P, 32], F32)
        REC = sb("REC", [P, 32], F32)
        RS = sb("RS", [P, 32], F32)
        NMS = sb("NMS", [P, 32], F32)
        RS2 = sb("RS2", [P, 32], F32)
        NMS2 = sb("NMS2", [P, 32], F32)
        dumpA = sb("dumpA", [P, D], F32)
        XTD = sb("XTD", [P, 2 * 2048], BF16)
        NTI1 = sb("NTI1", [P, 32], mybir.dt.int32)
        NTI2 = sb("NTI2", [P, 32], mybir.dt.int32)
        NTIY = sb("NTIY", [P, 32], mybir.dt.int32)
        NTYa = sb("NTYa", [P, 32], F32)
        NTYb = sb("NTYb", [P, 32], F32)
        NTA = sb("NTA", [P, 32], F32)
        NTB = sb("NTB", [P, 32], F32)
        NTH = sb("NTH", [P, 32], F32)

        s_j = [sem(f"s_j{k}") for k in range(10)]
        s_xw = [sem(f"s_xw{k}") for k in range(3)]
        s_sq = sem("s_sq"); s_vrec = sem("s_vrec"); s_rs = sem("s_rs")
        s_vnm = sem("s_vnm"); s_xn = sem("s_xn")
        s_wt = [sem(f"s_wt{k}") for k in range(2)]
        s_sg = sem("s_sg")
        s_wn = [sem(f"s_wn{k}") for k in range(2)]
        s_ab = sem("s_ab"); s_al1 = sem("s_al1"); s_al2 = sem("s_al2")
        s_b1l = sem("s_b1l"); s_a2w = sem("s_a2w"); s_a2l = sem("s_a2l")
        s_wqk = sem("s_wqk")
        s_trB = [sem(f"s_trB{k}") for k in range(2)]
        s_teT = sem("s_teT"); s_evX = sem("s_evX"); s_idl = sem("s_idl")
        s_xc = [sem(f"s_xc{k}") for k in range(4)]
        s_teK = sem("s_teK"); s_kev = sem("s_kev"); s_teMM = sem("s_teMM")
        s_teQ = sem("s_teQ"); s_qev = sem("s_qev"); s_mev = sem("s_mev")
        s_jr = [sem(f"s_jr{k}") for k in range(6)]
        s_cb = sem("s_cb")
        cbn = [0]
        s_xr = sem("s_xr"); s_vc1 = sem("s_vc1"); s_sq2 = sem("s_sq2")
        s_vrec2 = sem("s_vrec2"); s_rs2 = sem("s_rs2"); s_vnm2 = sem("s_vnm2")
        s_z = sem("s_z"); s_teC = sem("s_teC"); s_vq = sem("s_vq")
        s_gbc = sem("s_gbc"); s_cbc = sem("s_cbc"); s_abc = sem("s_abc")
        s_trD = [sem(f"s_trD{k}") for k in range(2)]
        s_mm1 = sem("s_mm1"); s_ge = sem("s_ge"); s_mm2 = sem("s_mm2")
        s_epu = sem("s_epu"); s_epv = sem("s_epv"); s_vw = sem("s_vw")
        s_vo = sem("s_vo")
        s_ow = [sem(f"s_ow{k}") for k in range(2)]


        wg_slot = {3 + 2 * g: g for g in range(16)}
        wn_slot = {4 + 2 * u: u for u in range(16)}

        with ExitStack() as sA:
            Jring = sA.enter_context(nc.sbuf_tensor("Jring", [P, 10 * D], BF16))
            XNr = sA.enter_context(nc.sbuf_tensor("XNr", [P, 3 * D], BF16))
            wTbf = sA.enter_context(nc.sbuf_tensor("wTbf", [P, 2 * 4096], BF16))
            sgnb = sA.enter_context(nc.sbuf_tensor("sgnb", [P, 4096], BF16))
            wnb = sA.enter_context(nc.sbuf_tensor("wnb", [P, 2 * 4096], BF16))
            XTe = sA.enter_context(nc.sbuf_tensor("XTe", [P, 4 * D], BF16))
            IDN = sA.enter_context(nc.sbuf_tensor("IDN", [P, P], BF16))
            XNc = sA.enter_context(nc.sbuf_tensor("XNc", [P, 4 * D], BF16))
            WQTb = sA.enter_context(nc.sbuf_tensor("WQTb", [P, D], BF16))
            WKTb = sA.enter_context(nc.sbuf_tensor("WKTb", [P, D], BF16))
            Kb = sA.enter_context(nc.sbuf_tensor("Kb", [P, 8 * 128], BF16))
            pK = [sA.enter_context(nc.psum_tensor(f"pK{x}", [P, 128], F32)) for x in range(2)]
            pQT = [sA.enter_context(nc.psum_tensor(f"pQT{x}", [P, 256], F32)) for x in range(2)]
            pM = sA.enter_context(nc.psum_tensor("pM", [P, D], F32))
            pT = sA.enter_context(nc.psum_tensor("pT", [P, D], F32))

            with nc.Block() as blk:
                @blk.gpsimd
                def _(gpsimd):
                    gpsimd.dma_start(IDN[:], diden.ap()).then_inc(s_idl, 16)

                    def wt_load(g):
                        base = (g % 2) * 4096
                        if g < 8:
                            gpsimd.dma_start(wTbf[:, base:base + 4096],
                                             dw1T[g * 128:(g + 1) * 128, :]
                                             ).then_inc(s_wt[g % 2], 16)
                        else:
                            t = g - 8
                            gpsimd.dma_start(
                                wTbf[:, base:base + 4096]
                                .rearrange("p (a b) -> p a b", b=D),
                                bass.AP(dw2T, t * 512 * D,
                                        [[D, 128], [128 * D, 4], [1, D]])
                                ).then_inc(s_wt[g % 2], 16)

                    def wn_load(u):
                        base = (u % 2) * 4096
                        if u < 8:
                            gpsimd.dma_start(
                                wnb[:, base:base + 4096]
                                .rearrange("p (a b) -> p a b", b=D),
                                bass.AP(dw1n, u * 512 * D,
                                        [[D, 128], [128 * D, 4], [1, D]])
                                ).then_inc(s_wn[u % 2], 16)
                        else:
                            t = u - 8
                            gpsimd.dma_start(wnb[:, base:base + 4096],
                                             dw2n[t * 128:(t + 1) * 128, :]
                                             ).then_inc(s_wn[u % 2], 16)

                    wt_load(0)
                    wt_load(1)
                    wn_load(0)
                    wn_load(1)
                    for c in range(4):
                        gpsimd.dma_start(Jring[:, (c % 10) * D:(c % 10 + 1) * D],
                                         dJ[c * 128:(c + 1) * 128, :]).then_inc(s_j[c % 10], 16)
                    for dc in range(DC):
                        gpsimd.dma_start(WQTb[:, dc * 128:(dc + 1) * 128],
                                         dwqT[dc * 128:(dc + 1) * 128, :]).then_inc(s_wqk, 16)
                    for dc in range(DC):
                        gpsimd.dma_start(WKTb[:, dc * 128:(dc + 1) * 128],
                                         dwkT[dc * 128:(dc + 1) * 128, :]).then_inc(s_wqk, 16)
                    for c in range(41):
                        if 4 <= c + 2 < NCH:
                            if c - 7 >= 1:
                                gpsimd.wait_ge(s_xn, c - 7)
                            gpsimd.dma_start(Jring[:, ((c + 2) % 10) * D:((c + 2) % 10 + 1) * D],
                                             dJ[(c + 2) * 128:(c + 3) * 128, :]
                                             ).then_inc(s_j[(c + 2) % 10], 16)
                        cx = c - 10
                        if 0 <= cx <= 29:
                            gpsimd.wait_ge(s_xw[cx % 3], 16 * (cx // 3 + 1))
                            if cx >= 4:
                                gpsimd.wait_ge(s_teMM, cx - 3)
                            gpsimd.dma_start(XNc[:, (cx % 4) * D:(cx % 4 + 1) * D],
                                             dxnat[cx * 128:(cx + 1) * 128, :]
                                             ).then_inc(s_xc[cx % 4], 16)
                        if c in wg_slot and wg_slot[c] + 2 < 16:
                            g = wg_slot[c] + 2
                            gpsimd.wait_ge(s_sg, g - 1)
                            wt_load(g)
                        if c in wn_slot and wn_slot[c] + 2 < 16:
                            u = wn_slot[c] + 2
                            gpsimd.wait_ge(s_ab, 4 * (u - 1))
                            wn_load(u)
                    for cx in range(30, NCH):
                        gpsimd.wait_ge(s_xw[cx % 3], 16 * (cx // 3 + 1))
                        gpsimd.wait_ge(s_teMM, cx - 3)
                        gpsimd.dma_start(XNc[:, (cx % 4) * D:(cx % 4 + 1) * D],
                                         dxnat[cx * 128:(cx + 1) * 128, :]
                                         ).then_inc(s_xc[cx % 4], 16)

                @blk.sync
                def _(sync):
                    sync.dma_start(b1P[:], db1P.ap()).then_inc(s_b1l, 16)
                    for c in range(41):
                        if 0 <= c - 8 < NCH:
                            cc = c - 8
                            sync.wait_ge(s_xn, cc + 1)
                            sync.dma_start(dxnat[cc * 128:(cc + 1) * 128, :],
                                           XNr[:, (cc % 3) * D:(cc % 3 + 1) * D]
                                           ).then_inc(s_xw[cc % 3], 16)
                        if c in (16, 24):
                            q = {16: 0, 24: 1}[c]
                            for k in range(3):
                                ndone = 2 * (q + 1)
                                cnt = (ndone - 1 - k) // 3 + 1 if ndone - 1 >= k else 0
                                if cnt > 0:
                                    sync.wait_ge(s_xw[k], 16 * cnt)
                            for dc in range(DC):
                                sync.dma_start(
                                    XTD[:, (q % 2) * 2048 + dc * 256:
                                           (q % 2) * 2048 + dc * 256 + 256],
                                    bass.AP(dxnat, q * 256 * D + dc * 128,
                                            [[D, 256], [1, 128]]),
                                    transpose=True).then_inc(s_trD[q % 2], 16)

                @blk.vector
                def _(vector):
                    vector.memset(onesF[:], 1.0)
                    for c in range(41):
                        if c < NCH:
                            vector.wait_ge(s_j[c % 10], 16 * (c // 10 + 1))
                            vector.reduce_sum(SU[:, c:c + 1],
                                              Jring[:, (c % 10) * D:(c % 10 + 1) * D],
                                              axis=AX)
                        if c in wg_slot:
                            g = wg_slot[c]
                            vector.wait_ge(s_wt[g % 2], 16 * (g // 2 + 1))
                            vector.tensor_scalar(sgnb[:], wTbf[:, (g % 2) * 4096:
                                                               (g % 2) * 4096 + 4096],
                                                 1e38, 1.0, ALU.mult, ALU.min)
                        if 0 <= c - 2 < NCH:
                            vector.wait_ge(s_sq, c - 1)
                            vector.tensor_scalar(E2E[:, c - 2:c - 1], SQ[:, c - 2:c - 1],
                                                 INV_D, EPS, ALU.mult, ALU.add)
                            vector.tensor_scalar(MU[:, c - 2:c - 1], SU[:, c - 2:c - 1],
                                                 INV_D, None, ALU.mult)
                            vector.tensor_scalar(NEGMU[:, c - 2:c - 1], SU[:, c - 2:c - 1],
                                                 -INV_D, None, ALU.mult)
                        elif c in wg_slot:
                            vector.memset(junk1[:, jctr[0] % 8:jctr[0] % 8 + 1], 0.0)
                            jctr[0] += 1
                            vector.memset(junk1[:, jctr[0] % 8:jctr[0] % 8 + 1], 0.0)
                            jctr[0] += 1
                        if c in wg_slot:
                            g = wg_slot[c]
                            if g < 8:
                                dst = S1T[:, g * H: g * H + 4096]
                            else:
                                dst = S2T[:, (g - 8) * 4096: (g - 8) * 4096 + 4096]
                            vector.tensor_scalar(dst, sgnb[:], -1.0, None,
                                                 ALU.max).then_inc(s_sg, 1)
                        if 0 <= c - 3 < NCH:
                            vector.scalar_tensor_tensor(VAR[:, c - 3:c - 2],
                                                        MU[:, c - 3:c - 2],
                                                        NEGMU[:, c - 3:c - 2],
                                                        E2E[:, c - 3:c - 2],
                                                        ALU.mult, ALU.add)
                        if 0 <= c - 4 < NCH:
                            vector.reciprocal(REC[:, c - 4:c - 3],
                                              VAR[:, c - 4:c - 3]).then_inc(s_vrec, 1)
                        if 0 <= c - 6 < NCH:
                            vector.wait_ge(s_rs, c - 5)
                            vector.tensor_tensor(NMS[:, c - 6:c - 5],
                                                 NEGMU[:, c - 6:c - 5],
                                                 RS[:, c - 6:c - 5],
                                                 ALU.mult).then_inc(s_vnm, 1)
                        if 0 <= c - 7 < NCH:
                            cc = c - 7
                            if cc >= 3:
                                vector.wait_ge(s_xw[cc % 3], 16 * (cc // 3))
                                vector.wait_ge(s_teT, cc - 2)
                            vector.tensor_scalar(XNr[:, (cc % 3) * D:(cc % 3 + 1) * D],
                                                 Jring[:, (cc % 10) * D:(cc % 10 + 1) * D],
                                                 RS[:, cc:cc + 1], NMS[:, cc:cc + 1],
                                                 ALU.mult, ALU.add).then_inc(s_xn, 1)
                        if c == 21:
                            vector.wait_ge(s_ab, 32)
                            vector.tensor_scalar(alpha1P[:], AL1[:], INV_D, None,
                                                 ALU.mult).then_inc(s_al1, 1)
                        if 10 <= c <= 40:
                            idx = c - 10
                            nb, half = idx // 2, idx % 2
                            cch = 2 * nb + half
                            vector.wait_ge(s_teK, cch + 1)
                            if cch >= 8:
                                vector.wait_ge(s_teMM, cch - 7)
                            vector.tensor_copy(Kb[:, (cch % 8) * 128:(cch % 8 + 1) * 128],
                                               pK[cch % 2][:]).then_inc(s_kev, 1)
                            if half == 1:
                                vector.wait_ge(s_teQ, nb + 1)
                                vector.tensor_copy(QT[:, nb * 256:(nb + 1) * 256],
                                                   pQT[nb % 2][:]).then_inc(s_qev, 1)
                        if c == 36:
                            vector.wait_ge(s_ab, 64)
                            vector.tensor_tensor(AL2q[:, 0:8], AL2p[:, 0:8],
                                                 AL2p[:, 8:16], ALU.add)
                        if c == 37:
                            vector.tensor_tensor(AL2q[:, 8:16], AL2p[:, 16:24],
                                                 AL2p[:, 24:32], ALU.add)
                        if c == 38:
                            vector.tensor_tensor(AL2r[:], AL2q[:, 0:8],
                                                 AL2q[:, 8:16], ALU.add)
                        if c == 39:
                            vector.tensor_scalar(AL2s[:], AL2r[:], INV_H, None,
                                                 ALU.mult).then_inc(s_al2, 1)
                    vector.wait_ge(s_teK, NCH)
                    vector.wait_ge(s_teMM, 24)
                    vector.tensor_copy(Kb[:, (31 % 8) * 128:(31 % 8 + 1) * 128],
                                       pK[31 % 2][:]).then_inc(s_kev, 1)
                    vector.wait_ge(s_teQ, 16)
                    vector.tensor_copy(QT[:, 15 * 256:16 * 256],
                                       pQT[15 % 2][:]).then_inc(s_qev, 1)
                    vector.wait_ge(s_teMM, NCH)
                    vector.tensor_copy(Msb[:], pM[:]).then_inc(s_mev, 1)

                @blk.scalar
                def _(scalar):
                    for c in range(41):
                        if 0 <= c - 5 < NCH:
                            scalar.wait_ge(s_vrec, c - 4)
                            scalar.sqrt(RS[:, c - 5:c - 4],
                                        REC[:, c - 5:c - 4]).then_inc(s_rs, 1)
                        if 0 <= c - 8 < NCH:
                            ec = c - 8
                            scalar.wait_ge(s_teT, ec + 1)
                            if ec >= 4:
                                scalar.wait_ge(s_teQ, (ec - 4) // 2 + 1)
                            scalar.activation(XTe[:, (ec % 4) * D:(ec % 4 + 1) * D],
                                              pT[:], ACT.Identity).then_inc(s_evX, 1)
                        if c < NCH:
                            scalar.wait_ge(s_j[c % 10], 16 * (c // 10 + 1))
                            scalar.activation(dumpA[:],
                                              Jring[:, (c % 10) * D:(c % 10 + 1) * D],
                                              ACT.Square,
                                              accum_out=SQ[:, c:c + 1]).then_inc(s_sq, 1)
                        if c in wn_slot:
                            u = wn_slot[c]
                            scalar.wait_ge(s_wn[u % 2], 16 * (u // 2 + 1))
                            base = (u % 2) * 4096
                            for k in range(4):
                                if u < 8:
                                    t = 4 * u + k
                                    tgt = AL1[:, (t % 4) * 8 + t // 4:
                                                 (t % 4) * 8 + t // 4 + 1]
                                else:
                                    tt = u - 8
                                    tgt = AL2p[:, k * 8 + tt: k * 8 + tt + 1]
                                scalar.activation(dumpA[:],
                                                  wnb[:, base + k * D:base + (k + 1) * D],
                                                  ACT.Abs, accum_out=tgt).then_inc(s_ab, 1)

                @blk.tensor
                def _(tensor):
                    tensor.wait_ge(s_wqk, 16 * 16)
                    tensor.wait_ge(s_idl, 16)
                    LM = 6

                    def do_m(mcc):
                        tensor.wait_ge(s_kev, mcc + 1)
                        tensor.wait_ge(s_xc[mcc % 4], 16 * (mcc // 4 + 1))
                        for hf in range(2):
                            mm = tensor.matmul(
                                pM[:, hf * 512:(hf + 1) * 512],
                                Kb[:, (mcc % 8) * 128:(mcc % 8 + 1) * 128],
                                XNc[:, (mcc % 4) * D + hf * 512:
                                       (mcc % 4) * D + hf * 512 + 512],
                                start=(mcc == 0), stop=(mcc == NCH - 1),
                                skip_group_check=True)
                        mm.then_inc(s_teMM, 1)

                    for cch in range(NCH):
                        tensor.wait_ge(s_xn, cch + 1)
                        if cch >= 1:
                            tensor.wait_ge(s_evX, cch)
                        for dc in range(DC):
                            mm = tensor.matmul(
                                pT[:, dc * 128:(dc + 1) * 128],
                                XNr[:, (cch % 3) * D + dc * 128:
                                       (cch % 3) * D + dc * 128 + 128],
                                IDN[:], start=True, stop=True,
                                skip_group_check=True)
                        mm.then_inc(s_teT, 1)
                        tensor.wait_ge(s_evX, cch + 1)
                        if cch >= 2:
                            tensor.wait_ge(s_kev, cch - 1)
                        for dc in range(DC):
                            mm = tensor.matmul(
                                pK[cch % 2][:],
                                XTe[:, (cch % 4) * D + dc * 128:
                                       (cch % 4) * D + dc * 128 + 128],
                                WKTb[:, dc * 128:(dc + 1) * 128],
                                start=(dc == 0), stop=(dc == DC - 1),
                                skip_group_check=True)
                        mm.then_inc(s_teK, 1)
                        if cch >= LM:
                            do_m(cch - LM)
                        if cch % 2 == 1:
                            nb = cch // 2
                            if nb >= 2:
                                tensor.wait_ge(s_qev, nb - 1)
                            for j in range(2):
                                for dc in range(DC):
                                    mm = tensor.matmul(
                                        pQT[nb % 2][:, j * 128:(j + 1) * 128],
                                        WQTb[:, dc * 128:(dc + 1) * 128],
                                        XTe[:, ((2 * nb + j) % 4) * D + dc * 128:
                                               ((2 * nb + j) % 4) * D + dc * 128 + 128],
                                        start=(dc == 0), stop=(dc == DC - 1),
                                        skip_group_check=True)
                            mm.then_inc(s_teQ, 1)
                    for mcc in range(NCH - LM, NCH):
                        do_m(mcc)

        with ExitStack() as sD:
            G = sD.enter_context(nc.sbuf_tensor("G", [P, HT * NB], BF16))
            a2BC = sD.enter_context(nc.sbuf_tensor("a2BC", [P, D], F32))
            CBC = sD.enter_context(nc.sbuf_tensor("CBC", [P, D], F32))
            a2row = sD.enter_context(nc.sbuf_tensor("a2row", [1, D], F32))
            gBC = sD.enter_context(nc.sbuf_tensor("gBC", [P, D], F32))
            Jr = sD.enter_context(nc.sbuf_tensor("Jr", [P, 6 * D], F32))
            xrb = sD.enter_context(nc.sbuf_tensor("xrb", [P, 2 * D], BF16))
            tbuf = sD.enter_context(nc.sbuf_tensor("tbuf", [P, 4 * D], BF16))
            zbuf = sD.enter_context(nc.sbuf_tensor("zbuf", [P, 2 * D], BF16))
            qbuf = sD.enter_context(nc.sbuf_tensor("qbuf", [P, 4 * D], BF16))
            ubuf = sD.enter_context(nc.sbuf_tensor("ubuf", [P, 2 * D], F32))
            vbuf = sD.enter_context(nc.sbuf_tensor("vbuf", [P, D], F32))
            wbuf = sD.enter_context(nc.sbuf_tensor("wbuf", [P, 2 * D], F32))
            ostg = sD.enter_context(nc.sbuf_tensor("ostg", [P, 2 * D], F32))
            pg = [sD.enter_context(nc.psum_tensor(f"pg{x}", [P, NB], F32)) for x in range(4)]
            po = [sD.enter_context(nc.psum_tensor(f"po{x}", [P, 512], F32)) for x in range(2)]
            pA = sD.enter_context(nc.psum_tensor("pA", [P, D], F32))
            with nc.Block() as blk:
                @blk.sync
                def _(sync):
                    sync.dma_start(gBC[:], dgB.ap()).then_inc(s_gbc, 16)
                    sync.dma_start(ubuf[:, 0:D], dbB.ap()).then_inc(s_cbc, 16)
                    sync.dma_start(wbuf[:, 0:D], db2B.ap()).then_inc(s_cbc, 16)
                    for i in range(6):
                        sync.dma_start(Jr[:, (i % 6) * D:(i % 6 + 1) * D],
                                       dJ[i * 128:(i + 1) * 128, :]).then_inc(s_jr[i % 6], 16)
                    sync.wait_ge(s_al2, 1)
                    sync.dma_start(bass.AP(da2, 0, [[1, 128], [128, 8]]), AL2s[:],
                                   allow_slow_non_contiguous=True).then_inc(s_a2w, 16)
                    sync.wait_ge(s_a2w, 16)
                    sync.dma_start(a2row[:], bass.AP(da2, 0, [[1, 1], [1, D]])
                                   ).then_inc(s_a2l, 16)
                    for nb in range(NBLK):
                        if nb + 2 < NBLK:
                            nbp = nb + 2
                            sync.wait_ge(s_mm1, HT * (nbp - 1))
                            for dc in range(DC):
                                sync.dma_start(
                                    XTD[:, (nbp % 2) * 2048 + dc * 256:
                                           (nbp % 2) * 2048 + dc * 256 + 256],
                                    bass.AP(dxnat, nbp * 256 * D + dc * 128,
                                            [[D, 256], [1, 128]]),
                                    transpose=True).then_inc(s_trD[nbp % 2], 16)
                        for st in range(NSUB):
                            i = nb * NSUB + st
                            if i + 6 < NT:
                                sync.wait_ge(s_vw, i + 1)
                                sync.dma_start(Jr[:, ((i + 6) % 6) * D:((i + 6) % 6 + 1) * D],
                                               dJ[(i + 6) * 128:(i + 7) * 128, :]
                                               ).then_inc(s_jr[(i + 6) % 6], 16)
                            if i - 3 >= 0:
                                ii = i - 3
                                sync.wait_ge(s_vo, ii + 1)
                                sync.dma_start(dout[ii * 128:(ii + 1) * 128, :],
                                               ostg[:, (ii % 2) * D:(ii % 2 + 1) * D]
                                               ).then_inc(s_ow[ii % 2], 16)
                    for ii in range(NT - 3, NT):
                        sync.wait_ge(s_vo, ii + 1)
                        sync.dma_start(dout[ii * 128:(ii + 1) * 128, :],
                                       ostg[:, (ii % 2) * D:(ii % 2 + 1) * D]
                                       ).then_inc(s_ow[ii % 2], 16)
                    sync.wait_ge(s_ow[0], 16 * (NT // 2))
                    sync.wait_ge(s_ow[1], 16 * (NT // 2))

                @blk.vector
                def _(vector):
                    def vb(ins):
                        ins.then_inc(s_cb, 1)
                        cbn[0] += 1
                        vector.wait_ge(s_cb, cbn[0])

                    def chain(i):
                        vector.wait_ge(s_jr[i % 6], 16 * (i // 6 + 1))
                        vb(vector.tensor_scalar(xrb[:, (i % 2) * D:(i % 2 + 1) * D],
                                                Jr[:, (i % 6) * D:(i % 6 + 1) * D],
                                                RS[:, i:i + 1], NMS[:, i:i + 1],
                                                ALU.mult, ALU.add))
                        vector.wait_ge(s_teC, i + 1)
                        vector.scalar_tensor_tensor(
                            tbuf[:, (i % 4) * D:(i % 4 + 1) * D],
                            pA[:], ISQ,
                            xrb[:, (i % 2) * D:(i % 2 + 1) * D],
                            ALU.mult, ALU.add,
                            accum_out=SU[:, i:i + 1]).then_inc(s_vc1, 1)
                        vector.wait_ge(s_vc1, i + 1)
                        vb(vector.scalar_tensor_tensor(
                            dumpA[:],
                            tbuf[:, (i % 4) * D:(i % 4 + 1) * D], 1.0,
                            tbuf[:, (i % 4) * D:(i % 4 + 1) * D],
                            ALU.mult, ALU.mult,
                            accum_out=SQ[:, i:i + 1]))
                        vector.tensor_scalar(E2E[:, i:i + 1], SQ[:, i:i + 1],
                                             INV_D, EPS, ALU.mult, ALU.add)
                        vector.tensor_scalar(MU[:, i:i + 1], SU[:, i:i + 1],
                                             INV_D, None, ALU.mult)
                        vb(vector.tensor_scalar(NEGMU[:, i:i + 1], SU[:, i:i + 1],
                                                -INV_D, None, ALU.mult))
                        vb(vector.scalar_tensor_tensor(VAR[:, i:i + 1],
                                                       MU[:, i:i + 1],
                                                       NEGMU[:, i:i + 1],
                                                       E2E[:, i:i + 1],
                                                       ALU.mult, ALU.add))
                        vi = VAR[:, i:i + 1].bitcast(mybir.dt.int32)
                        vb(vector.tensor_scalar(NTI1[:, i:i + 1], vi, 1, None,
                                                ALU.arith_shift_right))
                        vb(vector.tensor_scalar(NTI2[:, i:i + 1], NTI1[:, i:i + 1],
                                                -1, None, ALU.mult))
                        vb(vector.tensor_scalar(NTIY[:, i:i + 1], NTI2[:, i:i + 1],
                                                0x5f3759df, None, ALU.add))
                        ys = [NTIY[:, i:i + 1].bitcast(F32),
                              NTYa[:, i:i + 1], NTYb[:, i:i + 1]]
                        cur = ys[0]
                        for k in range(3):
                            nxt = ys[1 + (k % 2)] if k < 2 else NTYa[:, i:i + 1]
                            if k == 1:
                                nxt = NTYb[:, i:i + 1]
                            if k == 2:
                                nxt = NTYa[:, i:i + 1]
                            vb(vector.tensor_tensor(NTA[:, i:i + 1], cur, cur, ALU.mult))
                            vb(vector.tensor_tensor(NTB[:, i:i + 1], NTA[:, i:i + 1],
                                                    VAR[:, i:i + 1], ALU.mult))
                            vb(vector.tensor_scalar(NTH[:, i:i + 1], NTB[:, i:i + 1],
                                                    -0.5, 1.5, ALU.mult, ALU.add))
                            vb(vector.tensor_tensor(nxt, cur, NTH[:, i:i + 1], ALU.mult))
                            cur = nxt
                        vector.tensor_tensor(NMS2[:, i:i + 1],
                                             NEGMU[:, i:i + 1], cur,
                                             ALU.mult).then_inc(s_vnm2, 1)
                        vector.wait_ge(s_vnm2, i + 1)
                        if i >= 2:
                            vector.wait_ge(s_vq, i - 1)
                        vb(vector.tensor_scalar(zbuf[:, (i % 2) * D:(i % 2 + 1) * D],
                                                tbuf[:, (i % 4) * D:(i % 4 + 1) * D],
                                                NTYa[:, i:i + 1], NMS2[:, i:i + 1],
                                                ALU.mult, ALU.add))
                        vector.wait_ge(s_gbc, 16)
                        vector.tensor_tensor(qbuf[:, (i % 4) * D:(i % 4 + 1) * D],
                                             zbuf[:, (i % 2) * D:(i % 2 + 1) * D],
                                             gBC[:], ALU.mult).then_inc(s_vq, 1)

                    vector.wait_ge(s_abc, 1)
                    vector.tensor_copy(a2BC[:], pA[:]).then_inc(s_abc, 1)
                    vector.wait_ge(s_cbc, 32)
                    vector.tensor_tensor(CBC[:], ubuf[:, 0:D], wbuf[:, 0:D],
                                         ALU.add).then_inc(s_cbc, 1)
                    chain(0)
                    chain(1)
                    for slot in range(NT + 3):
                        if slot + 2 < NT:
                            chain(slot + 2)
                        i = slot
                        if i < NT:
                            for hf in range(2):
                                hidx = 2 * i + hf
                                vector.wait_ge(s_mm2, hidx + 1)
                                vector.tensor_tensor(
                                    ubuf[:, (i % 2) * D + hf * 512:
                                            (i % 2) * D + hf * 512 + 512],
                                    po[hidx % 2][:],
                                    a2BC[:, hf * 512:(hf + 1) * 512],
                                    ALU.mult).then_inc(s_epu, 1)
                        i2 = slot - 2
                        if 0 <= i2 < NT:
                            vector.wait_ge(s_jr[i2 % 6], 16 * (i2 // 6 + 1))
                            vector.tensor_tensor(wbuf[:, (i2 % 2) * D:(i2 % 2 + 1) * D],
                                                 vbuf[:],
                                                 Jr[:, (i2 % 6) * D:(i2 % 6 + 1) * D],
                                                 ALU.add).then_inc(s_vw, 1)
                        i1 = slot - 1
                        if 0 <= i1 < NT:
                            vector.wait_ge(s_vq, i1 + 1)
                            vector.tensor_tensor(vbuf[:],
                                                 ubuf[:, (i1 % 2) * D:(i1 % 2 + 1) * D],
                                                 qbuf[:, (i1 % 4) * D:(i1 % 4 + 1) * D],
                                                 ALU.add).then_inc(s_epv, 1)
                        i3 = slot - 3
                        if 0 <= i3 < NT:
                            if i3 >= 2:
                                vector.wait_ge(s_ow[i3 % 2], 16 * (i3 // 2))
                            vector.tensor_tensor(ostg[:, (i3 % 2) * D:(i3 % 2 + 1) * D],
                                                 wbuf[:, (i3 % 2) * D:(i3 % 2 + 1) * D],
                                                 CBC[:], ALU.add).then_inc(s_vo, 1)

                @blk.scalar
                def _(scalar):
                    scalar.wait_ge(s_al1, 1)
                    scalar.wait_ge(s_b1l, 16)
                    for nb in range(NBLK):
                        for ht in range(HT):
                            g_idx = nb * HT + ht
                            scalar.wait_ge(s_mm1, g_idx + 1)
                            a1c = (ht % 4) * 8 + ht // 4
                            scalar.activation(G[:, ht * NB:(ht + 1) * NB],
                                              pg[g_idx % 4][:], ACT.Gelu,
                                              bias=b1P[:, ht:ht + 1],
                                              scale=alpha1P[:, a1c:a1c + 1]).then_inc(s_ge, 1)

                @blk.tensor
                def _(tensor):
                    def pa_mm(i):
                        if i >= 1:
                            tensor.wait_ge(s_vc1, i)
                        for hf in range(2):
                            mm = tensor.matmul(
                                pA[:, hf * 512:(hf + 1) * 512],
                                QT[:, i * 128:(i + 1) * 128],
                                Msb[:, hf * 512:(hf + 1) * 512],
                                start=True, stop=True, skip_group_check=True)
                        mm.then_inc(s_teC, 1)

                    tensor.wait_ge(s_sg, 16)
                    for nb in range(NBLK):
                        xt0 = (nb % 2) * 2048
                        tensor.wait_ge(s_trD[nb % 2], 16 * 8 * (nb // 2 + 1))
                        for ht in range(HT):
                            g_idx = nb * HT + ht
                            if nb == 0 and ht == 8:
                                tensor.wait_ge(s_a2l, 16)
                                for hf in range(2):
                                    mm = tensor.matmul(
                                        pA[:, hf * 512:(hf + 1) * 512],
                                        onesF[:], a2row[:1, hf * 512:(hf + 1) * 512],
                                        start=True, stop=True, skip_group_check=True)
                                mm.then_inc(s_abc, 1)
                            if nb == 0 and ht == 12:
                                tensor.wait_ge(s_abc, 2)
                                tensor.wait_ge(s_mev, 1)
                                pa_mm(0)
                            if ht == 2 and nb >= 1 and 2 * nb + 1 < NT:
                                pa_mm(2 * nb + 1)
                            if ht == 16 and nb >= 1 and 2 * nb + 2 < NT:
                                pa_mm(2 * nb + 2)
                            if nb == 0 and ht == 20:
                                pa_mm(1)
                            if nb == 0 and ht == 24:
                                pa_mm(2)
                            if g_idx >= 4:
                                tensor.wait_ge(s_ge, g_idx - 3)
                            for dc in range(DC):
                                mm = tensor.matmul(
                                    pg[g_idx % 4][:],
                                    S1T[:, dc * H + ht * 128: dc * H + ht * 128 + 128],
                                    XTD[:, xt0 + dc * 256: xt0 + dc * 256 + 256],
                                    start=(dc == 0), stop=(dc == 7),
                                    skip_group_check=True)
                            mm.then_inc(s_mm1, 1)
                        tensor.wait_ge(s_ge, HT * (nb + 1))
                        for st in range(NSUB):
                            i = nb * NSUB + st
                            for hf in range(2):
                                hidx = 2 * i + hf
                                if hidx >= 2:
                                    tensor.wait_ge(s_epu, hidx - 1)
                                for hc in range(HT):
                                    mm = tensor.matmul(
                                        po[hidx % 2][:],
                                        G[:, hc * NB + st * 128: hc * NB + st * 128 + 128],
                                        S2T[:, hc * D + hf * 512: hc * D + hf * 512 + 512],
                                        start=(hc == 0), stop=(hc == HT - 1),
                                        skip_group_check=True)
                                mm.then_inc(s_mm2, 1)

    return nc


_NC_CACHE = None


def kernel(L0_attn_J, L0_attn_W_Q, L0_attn_W_K, L0_attn_gamma, L0_attn_beta,
           L0_ffn_w1, L0_ffn_b1, L0_ffn_w2, L0_ffn_b2):
    J = np.ascontiguousarray(np.asarray(L0_attn_J, dtype=np.float32))
    wq = np.asarray(L0_attn_W_Q, dtype=np.float32)
    wk = np.asarray(L0_attn_W_K, dtype=np.float32)
    gamma = np.asarray(L0_attn_gamma, dtype=np.float32)
    beta = np.asarray(L0_attn_beta, dtype=np.float32)
    w1 = np.ascontiguousarray(np.asarray(L0_ffn_w1, dtype=np.float32))
    b1 = np.asarray(L0_ffn_b1, dtype=np.float32)
    w2 = np.ascontiguousarray(np.asarray(L0_ffn_w2, dtype=np.float32))
    b2 = np.asarray(L0_ffn_b2, dtype=np.float32)

    wqT = np.ascontiguousarray(wq.T)
    wkT = np.ascontiguousarray(wk.T)
    w1T = np.ascontiguousarray(w1.T)
    w2T = np.ascontiguousarray(w2.T)
    gB = np.ascontiguousarray(np.broadcast_to(gamma[None, :], (P, D)))
    bB = np.ascontiguousarray(np.broadcast_to(beta[None, :], (P, D)))
    b2B = np.ascontiguousarray(np.broadcast_to(b2[None, :], (P, D)))
    b1P = np.ascontiguousarray(b1.reshape(32, 128).T)
    ident = np.eye(P, dtype=np.float32)

    global _NC_CACHE
    if _NC_CACHE is None:
        _NC_CACHE = build_nc()
    nc = _NC_CACHE

    in_maps = []
    for b in range(8):
        in_maps.append({
            "J": np.ascontiguousarray(J[b]),
            "wqT": wqT, "wkT": wkT, "w1T": w1T, "w2T": w2T,
            "w1n": w1, "w2n": w2,
            "gB": gB, "bB": bB, "b2B": b2B, "b1P": b1P,
            "iden": ident,
        })
    res = run_bass_kernel_spmd(nc, in_maps, core_ids=list(range(8)))
    out = np.stack([np.asarray(r["out"], dtype=np.float32) for r in res.results], axis=0)
    return out


# version 12
# speedup vs baseline: 1.0461x; 1.0058x over previous
import math
import numpy as np
from contextlib import ExitStack

import concourse.bass as bass
import concourse.mybir as mybir
from concourse.bass_utils import run_bass_kernel_spmd

F32 = mybir.dt.float32
BF16 = mybir.dt.bfloat16
FP8 = mybir.dt.float8e4
ALU = mybir.AluOpType
ACT = mybir.ActivationFunctionType
AX = mybir.AxisListType.X

P = 128
D = 1024
H = 4096
N = 4096
NCH = 32
DC = 8
HT = 32
NB = 256
NBLK = 16
NSUB = 2
NT = NBLK * NSUB
INV_D = 1.0 / 1024.0
INV_H = 1.0 / 4096.0
EPS = 1e-5
ISQ = 1.0 / math.sqrt(128.0)


def build_nc():
    nc = bass.Bass("TRN2", target_bir_lowering=False, detect_race_conditions=False)

    dJ = nc.dram_tensor("J", [N, D], F32, kind="ExternalInput")
    dwqT = nc.dram_tensor("wqT", [D, 128], F32, kind="ExternalInput")
    dwkT = nc.dram_tensor("wkT", [D, 128], F32, kind="ExternalInput")
    dw1T = nc.dram_tensor("w1T", [D, H], F32, kind="ExternalInput")
    dw2T = nc.dram_tensor("w2T", [H, D], F32, kind="ExternalInput")
    dw1n = nc.dram_tensor("w1n", [H, D], F32, kind="ExternalInput")
    dw2n = nc.dram_tensor("w2n", [D, H], F32, kind="ExternalInput")
    dgB = nc.dram_tensor("gB", [P, D], F32, kind="ExternalInput")
    dbB = nc.dram_tensor("bB", [P, D], F32, kind="ExternalInput")
    db2B = nc.dram_tensor("b2B", [P, D], F32, kind="ExternalInput")
    db1P = nc.dram_tensor("b1P", [P, 32], F32, kind="ExternalInput")
    diden = nc.dram_tensor("iden", [P, P], F32, kind="ExternalInput")
    dout = nc.dram_tensor("out", [N, D], F32, kind="ExternalOutput")
    dxnat = nc.dram_tensor("xnat", [N, D], BF16, kind="Internal")
    da2 = nc.dram_tensor("a2scr", [D], F32, kind="Internal")

    ctx = ExitStack()
    sb = lambda n, s, d: ctx.enter_context(nc.sbuf_tensor(n, s, d))
    sem = lambda n: ctx.enter_context(nc.semaphore(n))

    with ctx:
        S1T = sb("S1T", [P, DC * H], FP8)
        S2T = sb("S2T", [P, HT * D], FP8)
        QT = sb("QT", [P, N], BF16)
        Msb = sb("Msb", [P, D], BF16)
        alpha1P = sb("alpha1P", [P, 32], F32)
        AL1 = sb("AL1", [P, 32], F32)
        AL2p = sb("AL2p", [P, 32], F32)
        AL2q = sb("AL2q", [P, 16], F32)
        AL2r = sb("AL2r", [P, 8], F32)
        AL2s = sb("AL2s", [P, 8], F32)
        b1P = sb("b1Ps", [P, 32], F32)
        onesF = sb("onesF", [1, P], F32)
        junk1 = sb("junk1", [P, 8], F32)
        jctr = [0]
        SU = sb("SU", [P, 32], F32)
        SQ = sb("SQ", [P, 32], F32)
        MU = sb("MU", [P, 32], F32)
        E2E = sb("E2E", [P, 32], F32)
        NEGMU = sb("NEGMU", [P, 32], F32)
        VAR = sb("VAR", [P, 32], F32)
        REC = sb("REC", [P, 32], F32)
        RS = sb("RS", [P, 32], F32)
        NMS = sb("NMS", [P, 32], F32)
        RS2 = sb("RS2", [P, 32], F32)
        NMS2 = sb("NMS2", [P, 32], F32)
        dumpA = sb("dumpA", [P, D], F32)
        XTD = sb("XTD", [P, 2 * 2048], BF16)
        NTI1 = sb("NTI1", [P, 32], mybir.dt.int32)
        NTI2 = sb("NTI2", [P, 32], mybir.dt.int32)
        NTIY = sb("NTIY", [P, 32], mybir.dt.int32)
        NTYa = sb("NTYa", [P, 32], F32)
        NTYb = sb("NTYb", [P, 32], F32)
        NTA = sb("NTA", [P, 32], F32)
        NTB = sb("NTB", [P, 32], F32)
        NTH = sb("NTH", [P, 32], F32)

        s_j = [sem(f"s_j{k}") for k in range(10)]
        s_xw = [sem(f"s_xw{k}") for k in range(3)]
        s_sq = sem("s_sq"); s_vrec = sem("s_vrec"); s_rs = sem("s_rs")
        s_vnm = sem("s_vnm"); s_xn = sem("s_xn")
        s_wt = [sem(f"s_wt{k}") for k in range(2)]
        s_sg = sem("s_sg")
        s_wn = [sem(f"s_wn{k}") for k in range(2)]
        s_ab = sem("s_ab"); s_al1 = sem("s_al1"); s_al2 = sem("s_al2")
        s_b1l = sem("s_b1l"); s_a2w = sem("s_a2w"); s_a2l = sem("s_a2l")
        s_wqk = sem("s_wqk")
        s_trB = [sem(f"s_trB{k}") for k in range(2)]
        s_teT = sem("s_teT"); s_evX = sem("s_evX"); s_idl = sem("s_idl")
        s_xc = [sem(f"s_xc{k}") for k in range(4)]
        s_teK = sem("s_teK"); s_kev = sem("s_kev"); s_teMM = sem("s_teMM")
        s_teQ = sem("s_teQ"); s_qev = sem("s_qev"); s_mev = sem("s_mev")
        s_jr = [sem(f"s_jr{k}") for k in range(6)]
        s_cb = sem("s_cb")
        cbn = [0]
        s_xr = sem("s_xr"); s_vc1 = sem("s_vc1"); s_sq2 = sem("s_sq2")
        s_vrec2 = sem("s_vrec2"); s_rs2 = sem("s_rs2"); s_vnm2 = sem("s_vnm2")
        s_z = sem("s_z"); s_teC = sem("s_teC"); s_vq = sem("s_vq")
        s_gbc = sem("s_gbc"); s_cbc = sem("s_cbc"); s_abc = sem("s_abc")
        s_trD = [sem(f"s_trD{k}") for k in range(2)]
        s_mm1 = sem("s_mm1"); s_ge = sem("s_ge"); s_mm2 = sem("s_mm2")
        s_epu = sem("s_epu"); s_epv = sem("s_epv"); s_vw = sem("s_vw")
        s_vo = sem("s_vo")
        s_ow = [sem(f"s_ow{k}") for k in range(2)]


        wg_slot = {3 + 2 * g: g for g in range(16)}
        wn_slot = {4 + 2 * u: u for u in range(16)}

        with ExitStack() as sA:
            Jring = sA.enter_context(nc.sbuf_tensor("Jring", [P, 10 * D], BF16))
            XNr = sA.enter_context(nc.sbuf_tensor("XNr", [P, 3 * D], BF16))
            wTbf = sA.enter_context(nc.sbuf_tensor("wTbf", [P, 2 * 4096], BF16))
            sgnb = sA.enter_context(nc.sbuf_tensor("sgnb", [P, 4096], BF16))
            wnb = sA.enter_context(nc.sbuf_tensor("wnb", [P, 2 * 4096], BF16))
            XTe = sA.enter_context(nc.sbuf_tensor("XTe", [P, 4 * D], BF16))
            IDN = sA.enter_context(nc.sbuf_tensor("IDN", [P, P], BF16))
            XNc = sA.enter_context(nc.sbuf_tensor("XNc", [P, 4 * D], BF16))
            WQTb = sA.enter_context(nc.sbuf_tensor("WQTb", [P, D], BF16))
            WKTb = sA.enter_context(nc.sbuf_tensor("WKTb", [P, D], BF16))
            Kb = sA.enter_context(nc.sbuf_tensor("Kb", [P, 8 * 128], BF16))
            pK = [sA.enter_context(nc.psum_tensor(f"pK{x}", [P, 128], F32)) for x in range(2)]
            pQT = [sA.enter_context(nc.psum_tensor(f"pQT{x}", [P, 256], F32)) for x in range(2)]
            pM = sA.enter_context(nc.psum_tensor("pM", [P, D], F32))
            pT = sA.enter_context(nc.psum_tensor("pT", [P, D], F32))

            with nc.Block() as blk:
                @blk.gpsimd
                def _(gpsimd):
                    gpsimd.dma_start(IDN[:], diden.ap()).then_inc(s_idl, 16)

                    def wt_load(g):
                        base = (g % 2) * 4096
                        if g < 8:
                            gpsimd.dma_start(wTbf[:, base:base + 4096],
                                             dw1T[g * 128:(g + 1) * 128, :]
                                             ).then_inc(s_wt[g % 2], 16)
                        else:
                            t = g - 8
                            gpsimd.dma_start(
                                wTbf[:, base:base + 4096]
                                .rearrange("p (a b) -> p a b", b=D),
                                bass.AP(dw2T, t * 512 * D,
                                        [[D, 128], [128 * D, 4], [1, D]])
                                ).then_inc(s_wt[g % 2], 16)

                    def wn_load(u):
                        base = (u % 2) * 4096
                        if u < 8:
                            gpsimd.dma_start(
                                wnb[:, base:base + 4096]
                                .rearrange("p (a b) -> p a b", b=D),
                                bass.AP(dw1n, u * 512 * D,
                                        [[D, 128], [128 * D, 4], [1, D]])
                                ).then_inc(s_wn[u % 2], 16)
                        else:
                            t = u - 8
                            gpsimd.dma_start(wnb[:, base:base + 4096],
                                             dw2n[t * 128:(t + 1) * 128, :]
                                             ).then_inc(s_wn[u % 2], 16)

                    wt_load(0)
                    wt_load(1)
                    wn_load(0)
                    wn_load(1)
                    for c in range(4):
                        gpsimd.dma_start(Jring[:, (c % 10) * D:(c % 10 + 1) * D],
                                         dJ[c * 128:(c + 1) * 128, :]).then_inc(s_j[c % 10], 16)
                    for dc in range(DC):
                        gpsimd.dma_start(WQTb[:, dc * 128:(dc + 1) * 128],
                                         dwqT[dc * 128:(dc + 1) * 128, :]).then_inc(s_wqk, 16)
                    for dc in range(DC):
                        gpsimd.dma_start(WKTb[:, dc * 128:(dc + 1) * 128],
                                         dwkT[dc * 128:(dc + 1) * 128, :]).then_inc(s_wqk, 16)
                    for c in range(41):
                        if 4 <= c + 2 < NCH:
                            if c - 7 >= 1:
                                gpsimd.wait_ge(s_xn, c - 7)
                            gpsimd.dma_start(Jring[:, ((c + 2) % 10) * D:((c + 2) % 10 + 1) * D],
                                             dJ[(c + 2) * 128:(c + 3) * 128, :]
                                             ).then_inc(s_j[(c + 2) % 10], 16)
                        cx = c - 10
                        if 0 <= cx <= 29:
                            gpsimd.wait_ge(s_xw[cx % 3], 16 * (cx // 3 + 1))
                            if cx >= 4:
                                gpsimd.wait_ge(s_teMM, cx - 3)
                            gpsimd.dma_start(XNc[:, (cx % 4) * D:(cx % 4 + 1) * D],
                                             dxnat[cx * 128:(cx + 1) * 128, :]
                                             ).then_inc(s_xc[cx % 4], 16)
                        if c in wg_slot and wg_slot[c] + 2 < 16:
                            g = wg_slot[c] + 2
                            gpsimd.wait_ge(s_sg, g - 1)
                            wt_load(g)
                        if c in wn_slot and wn_slot[c] + 2 < 16:
                            u = wn_slot[c] + 2
                            gpsimd.wait_ge(s_ab, 4 * (u - 1))
                            wn_load(u)
                    for cx in range(30, NCH):
                        gpsimd.wait_ge(s_xw[cx % 3], 16 * (cx // 3 + 1))
                        gpsimd.wait_ge(s_teMM, cx - 3)
                        gpsimd.dma_start(XNc[:, (cx % 4) * D:(cx % 4 + 1) * D],
                                         dxnat[cx * 128:(cx + 1) * 128, :]
                                         ).then_inc(s_xc[cx % 4], 16)

                @blk.sync
                def _(sync):
                    sync.dma_start(b1P[:], db1P.ap()).then_inc(s_b1l, 16)
                    for c in range(41):
                        if 0 <= c - 8 < NCH:
                            cc = c - 8
                            sync.wait_ge(s_xn, cc + 1)
                            sync.dma_start(dxnat[cc * 128:(cc + 1) * 128, :],
                                           XNr[:, (cc % 3) * D:(cc % 3 + 1) * D]
                                           ).then_inc(s_xw[cc % 3], 16)
                        if c in (16, 24):
                            q = {16: 0, 24: 1}[c]
                            for k in range(3):
                                ndone = 2 * (q + 1)
                                cnt = (ndone - 1 - k) // 3 + 1 if ndone - 1 >= k else 0
                                if cnt > 0:
                                    sync.wait_ge(s_xw[k], 16 * cnt)
                            for dc in range(DC):
                                sync.dma_start(
                                    XTD[:, (q % 2) * 2048 + dc * 256:
                                           (q % 2) * 2048 + dc * 256 + 256],
                                    bass.AP(dxnat, q * 256 * D + dc * 128,
                                            [[D, 256], [1, 128]]),
                                    transpose=True).then_inc(s_trD[q % 2], 16)

                @blk.vector
                def _(vector):
                    vector.memset(onesF[:], 1.0)
                    for c in range(41):
                        if c < NCH:
                            vector.wait_ge(s_j[c % 10], 16 * (c // 10 + 1))
                            vector.reduce_sum(SU[:, c:c + 1],
                                              Jring[:, (c % 10) * D:(c % 10 + 1) * D],
                                              axis=AX)
                        if c in wg_slot:
                            g = wg_slot[c]
                            vector.wait_ge(s_wt[g % 2], 16 * (g // 2 + 1))
                            vector.tensor_scalar(sgnb[:], wTbf[:, (g % 2) * 4096:
                                                               (g % 2) * 4096 + 4096],
                                                 1e38, 1.0, ALU.mult, ALU.min)
                        if 0 <= c - 2 < NCH:
                            vector.wait_ge(s_sq, c - 1)
                            vector.tensor_scalar(E2E[:, c - 2:c - 1], SQ[:, c - 2:c - 1],
                                                 INV_D, EPS, ALU.mult, ALU.add)
                            vector.tensor_scalar(MU[:, c - 2:c - 1], SU[:, c - 2:c - 1],
                                                 INV_D, None, ALU.mult)
                            vector.tensor_scalar(NEGMU[:, c - 2:c - 1], SU[:, c - 2:c - 1],
                                                 -INV_D, None, ALU.mult)
                        elif c in wg_slot:
                            vector.memset(junk1[:, jctr[0] % 8:jctr[0] % 8 + 1], 0.0)
                            jctr[0] += 1
                            vector.memset(junk1[:, jctr[0] % 8:jctr[0] % 8 + 1], 0.0)
                            jctr[0] += 1
                        if c in wg_slot:
                            g = wg_slot[c]
                            if g < 8:
                                dst = S1T[:, g * H: g * H + 4096]
                            else:
                                dst = S2T[:, (g - 8) * 4096: (g - 8) * 4096 + 4096]
                            vector.tensor_scalar(dst, sgnb[:], -1.0, None,
                                                 ALU.max).then_inc(s_sg, 1)
                        if 0 <= c - 3 < NCH:
                            vector.scalar_tensor_tensor(VAR[:, c - 3:c - 2],
                                                        MU[:, c - 3:c - 2],
                                                        NEGMU[:, c - 3:c - 2],
                                                        E2E[:, c - 3:c - 2],
                                                        ALU.mult, ALU.add)
                        if 0 <= c - 4 < NCH:
                            vector.reciprocal(REC[:, c - 4:c - 3],
                                              VAR[:, c - 4:c - 3]).then_inc(s_vrec, 1)
                        if 0 <= c - 6 < NCH:
                            vector.wait_ge(s_rs, c - 5)
                            vector.tensor_tensor(NMS[:, c - 6:c - 5],
                                                 NEGMU[:, c - 6:c - 5],
                                                 RS[:, c - 6:c - 5],
                                                 ALU.mult).then_inc(s_vnm, 1)
                        if 0 <= c - 7 < NCH:
                            cc = c - 7
                            if cc >= 3:
                                vector.wait_ge(s_xw[cc % 3], 16 * (cc // 3))
                                vector.wait_ge(s_teT, cc - 2)
                            pass
                            vector.tensor_scalar(XNr[:, (cc % 3) * D:(cc % 3 + 1) * D],
                                                 Jring[:, (cc % 10) * D:(cc % 10 + 1) * D],
                                                 RS[:, cc:cc + 1], NMS[:, cc:cc + 1],
                                                 ALU.mult, ALU.add).then_inc(s_xn, 1)
                        if 0 <= c - 8 < NCH:
                            ec = c - 8
                            vector.wait_ge(s_teT, ec + 1)
                            if ec >= 4:
                                vector.wait_ge(s_teQ, (ec - 4) // 2 + 1)
                            vector.tensor_copy(XTe[:, (ec % 4) * D:(ec % 4 + 1) * D],
                                               pT[:]).then_inc(s_evX, 1)
                        if c == 21:
                            vector.wait_ge(s_ab, 32)
                            vector.tensor_scalar(alpha1P[:], AL1[:], INV_D, None,
                                                 ALU.mult).then_inc(s_al1, 1)
                        if 10 <= c <= 40:
                            idx = c - 10
                            nb, half = idx // 2, idx % 2
                            cch = 2 * nb + half
                            vector.wait_ge(s_teK, cch + 1)
                            if cch >= 8:
                                vector.wait_ge(s_teMM, cch - 7)
                            vector.tensor_copy(Kb[:, (cch % 8) * 128:(cch % 8 + 1) * 128],
                                               pK[cch % 2][:]).then_inc(s_kev, 1)
                            if half == 1:
                                vector.wait_ge(s_teQ, nb + 1)
                                vector.tensor_copy(QT[:, nb * 256:(nb + 1) * 256],
                                                   pQT[nb % 2][:]).then_inc(s_qev, 1)
                        if c == 36:
                            vector.wait_ge(s_ab, 64)
                            vector.tensor_tensor(AL2q[:, 0:8], AL2p[:, 0:8],
                                                 AL2p[:, 8:16], ALU.add)
                        if c == 37:
                            vector.tensor_tensor(AL2q[:, 8:16], AL2p[:, 16:24],
                                                 AL2p[:, 24:32], ALU.add)
                        if c == 38:
                            vector.tensor_tensor(AL2r[:], AL2q[:, 0:8],
                                                 AL2q[:, 8:16], ALU.add)
                        if c == 39:
                            vector.tensor_scalar(AL2s[:], AL2r[:], INV_H, None,
                                                 ALU.mult).then_inc(s_al2, 1)
                    vector.wait_ge(s_teK, NCH)
                    vector.wait_ge(s_teMM, 24)
                    vector.tensor_copy(Kb[:, (31 % 8) * 128:(31 % 8 + 1) * 128],
                                       pK[31 % 2][:]).then_inc(s_kev, 1)
                    vector.wait_ge(s_teQ, 16)
                    vector.tensor_copy(QT[:, 15 * 256:16 * 256],
                                       pQT[15 % 2][:]).then_inc(s_qev, 1)
                    vector.wait_ge(s_teMM, NCH)
                    vector.tensor_copy(Msb[:], pM[:]).then_inc(s_mev, 1)

                @blk.scalar
                def _(scalar):
                    for c in range(41):
                        if 0 <= c - 5 < NCH:
                            scalar.wait_ge(s_vrec, c - 4)
                            scalar.sqrt(RS[:, c - 5:c - 4],
                                        REC[:, c - 5:c - 4]).then_inc(s_rs, 1)
                        if c < NCH:
                            scalar.wait_ge(s_j[c % 10], 16 * (c // 10 + 1))
                            scalar.activation(dumpA[:],
                                              Jring[:, (c % 10) * D:(c % 10 + 1) * D],
                                              ACT.Square,
                                              accum_out=SQ[:, c:c + 1]).then_inc(s_sq, 1)
                        if c in wn_slot:
                            u = wn_slot[c]
                            scalar.wait_ge(s_wn[u % 2], 16 * (u // 2 + 1))
                            base = (u % 2) * 4096
                            for k in range(4):
                                if u < 8:
                                    t = 4 * u + k
                                    tgt = AL1[:, (t % 4) * 8 + t // 4:
                                                 (t % 4) * 8 + t // 4 + 1]
                                else:
                                    tt = u - 8
                                    tgt = AL2p[:, k * 8 + tt: k * 8 + tt + 1]
                                scalar.activation(dumpA[:],
                                                  wnb[:, base + k * D:base + (k + 1) * D],
                                                  ACT.Abs, accum_out=tgt).then_inc(s_ab, 1)

                @blk.tensor
                def _(tensor):
                    tensor.wait_ge(s_wqk, 16 * 16)
                    tensor.wait_ge(s_idl, 16)
                    LM = 6

                    def do_m(mcc):
                        tensor.wait_ge(s_kev, mcc + 1)
                        tensor.wait_ge(s_xc[mcc % 4], 16 * (mcc // 4 + 1))
                        for hf in range(2):
                            mm = tensor.matmul(
                                pM[:, hf * 512:(hf + 1) * 512],
                                Kb[:, (mcc % 8) * 128:(mcc % 8 + 1) * 128],
                                XNc[:, (mcc % 4) * D + hf * 512:
                                       (mcc % 4) * D + hf * 512 + 512],
                                start=(mcc == 0), stop=(mcc == NCH - 1),
                                skip_group_check=True)
                        mm.then_inc(s_teMM, 1)

                    for cch in range(NCH):
                        tensor.wait_ge(s_xn, cch + 1)
                        if cch >= 1:
                            tensor.wait_ge(s_evX, cch)
                        for dc in range(DC):
                            mm = tensor.matmul(
                                pT[:, dc * 128:(dc + 1) * 128],
                                XNr[:, (cch % 3) * D + dc * 128:
                                       (cch % 3) * D + dc * 128 + 128],
                                IDN[:], start=True, stop=True,
                                skip_group_check=True)
                        mm.then_inc(s_teT, 1)
                        tensor.wait_ge(s_evX, cch + 1)
                        if cch >= 2:
                            tensor.wait_ge(s_kev, cch - 1)
                        for dc in range(DC):
                            mm = tensor.matmul(
                                pK[cch % 2][:],
                                XTe[:, (cch % 4) * D + dc * 128:
                                       (cch % 4) * D + dc * 128 + 128],
                                WKTb[:, dc * 128:(dc + 1) * 128],
                                start=(dc == 0), stop=(dc == DC - 1),
                                skip_group_check=True)
                        mm.then_inc(s_teK, 1)
                        if cch >= LM:
                            do_m(cch - LM)
                        if cch % 2 == 1:
                            nb = cch // 2
                            if nb >= 2:
                                tensor.wait_ge(s_qev, nb - 1)
                            for j in range(2):
                                for dc in range(DC):
                                    mm = tensor.matmul(
                                        pQT[nb % 2][:, j * 128:(j + 1) * 128],
                                        WQTb[:, dc * 128:(dc + 1) * 128],
                                        XTe[:, ((2 * nb + j) % 4) * D + dc * 128:
                                               ((2 * nb + j) % 4) * D + dc * 128 + 128],
                                        start=(dc == 0), stop=(dc == DC - 1),
                                        skip_group_check=True)
                            mm.then_inc(s_teQ, 1)
                    for mcc in range(NCH - LM, NCH):
                        do_m(mcc)

        with ExitStack() as sD:
            G = sD.enter_context(nc.sbuf_tensor("G", [P, HT * NB], BF16))
            a2BC = sD.enter_context(nc.sbuf_tensor("a2BC", [P, D], F32))
            CBC = sD.enter_context(nc.sbuf_tensor("CBC", [P, D], F32))
            a2row = sD.enter_context(nc.sbuf_tensor("a2row", [1, D], F32))
            gBC = sD.enter_context(nc.sbuf_tensor("gBC", [P, D], F32))
            Jr = sD.enter_context(nc.sbuf_tensor("Jr", [P, 6 * D], F32))
            xrb = sD.enter_context(nc.sbuf_tensor("xrb", [P, 2 * D], BF16))
            tbuf = sD.enter_context(nc.sbuf_tensor("tbuf", [P, 4 * D], BF16))
            zbuf = sD.enter_context(nc.sbuf_tensor("zbuf", [P, 2 * D], BF16))
            qbuf = sD.enter_context(nc.sbuf_tensor("qbuf", [P, 4 * D], BF16))
            ubuf = sD.enter_context(nc.sbuf_tensor("ubuf", [P, 2 * D], F32))
            vbuf = sD.enter_context(nc.sbuf_tensor("vbuf", [P, D], F32))
            wbuf = sD.enter_context(nc.sbuf_tensor("wbuf", [P, 2 * D], F32))
            ostg = sD.enter_context(nc.sbuf_tensor("ostg", [P, 2 * D], F32))
            pg = [sD.enter_context(nc.psum_tensor(f"pg{x}", [P, NB], F32)) for x in range(4)]
            po = [sD.enter_context(nc.psum_tensor(f"po{x}", [P, 512], F32)) for x in range(2)]
            pA = sD.enter_context(nc.psum_tensor("pA", [P, D], F32))
            with nc.Block() as blk:
                @blk.sync
                def _(sync):
                    sync.dma_start(gBC[:], dgB.ap()).then_inc(s_gbc, 16)
                    sync.dma_start(ubuf[:, 0:D], dbB.ap()).then_inc(s_cbc, 16)
                    sync.dma_start(wbuf[:, 0:D], db2B.ap()).then_inc(s_cbc, 16)
                    for i in range(6):
                        sync.dma_start(Jr[:, (i % 6) * D:(i % 6 + 1) * D],
                                       dJ[i * 128:(i + 1) * 128, :]).then_inc(s_jr[i % 6], 16)
                    sync.wait_ge(s_al2, 1)
                    sync.dma_start(bass.AP(da2, 0, [[1, 128], [128, 8]]), AL2s[:],
                                   allow_slow_non_contiguous=True).then_inc(s_a2w, 16)
                    sync.wait_ge(s_a2w, 16)
                    sync.dma_start(a2row[:], bass.AP(da2, 0, [[1, 1], [1, D]])
                                   ).then_inc(s_a2l, 16)
                    for nb in range(NBLK):
                        if nb + 2 < NBLK:
                            nbp = nb + 2
                            sync.wait_ge(s_mm1, HT * (nbp - 1))
                            for dc in range(DC):
                                sync.dma_start(
                                    XTD[:, (nbp % 2) * 2048 + dc * 256:
                                           (nbp % 2) * 2048 + dc * 256 + 256],
                                    bass.AP(dxnat, nbp * 256 * D + dc * 128,
                                            [[D, 256], [1, 128]]),
                                    transpose=True).then_inc(s_trD[nbp % 2], 16)
                        for st in range(NSUB):
                            i = nb * NSUB + st
                            if i + 6 < NT:
                                sync.wait_ge(s_vw, i + 1)
                                sync.dma_start(Jr[:, ((i + 6) % 6) * D:((i + 6) % 6 + 1) * D],
                                               dJ[(i + 6) * 128:(i + 7) * 128, :]
                                               ).then_inc(s_jr[(i + 6) % 6], 16)
                            if i - 3 >= 0:
                                ii = i - 3
                                sync.wait_ge(s_vo, ii + 1)
                                sync.dma_start(dout[ii * 128:(ii + 1) * 128, :],
                                               ostg[:, (ii % 2) * D:(ii % 2 + 1) * D]
                                               ).then_inc(s_ow[ii % 2], 16)
                    for ii in range(NT - 3, NT):
                        sync.wait_ge(s_vo, ii + 1)
                        sync.dma_start(dout[ii * 128:(ii + 1) * 128, :],
                                       ostg[:, (ii % 2) * D:(ii % 2 + 1) * D]
                                       ).then_inc(s_ow[ii % 2], 16)
                    sync.wait_ge(s_ow[0], 16 * (NT // 2))
                    sync.wait_ge(s_ow[1], 16 * (NT // 2))

                @blk.vector
                def _(vector):
                    def vb(ins):
                        ins.then_inc(s_cb, 1)
                        cbn[0] += 1
                        vector.wait_ge(s_cb, cbn[0])

                    def chain(i):
                        vector.wait_ge(s_jr[i % 6], 16 * (i // 6 + 1))
                        vb(vector.tensor_scalar(xrb[:, (i % 2) * D:(i % 2 + 1) * D],
                                                Jr[:, (i % 6) * D:(i % 6 + 1) * D],
                                                RS[:, i:i + 1], NMS[:, i:i + 1],
                                                ALU.mult, ALU.add))
                        vector.wait_ge(s_teC, i + 1)
                        vector.scalar_tensor_tensor(
                            tbuf[:, (i % 4) * D:(i % 4 + 1) * D],
                            pA[:], ISQ,
                            xrb[:, (i % 2) * D:(i % 2 + 1) * D],
                            ALU.mult, ALU.add,
                            accum_out=SU[:, i:i + 1]).then_inc(s_vc1, 1)
                        vector.wait_ge(s_vc1, i + 1)
                        vb(vector.scalar_tensor_tensor(
                            dumpA[:],
                            tbuf[:, (i % 4) * D:(i % 4 + 1) * D], 1.0,
                            tbuf[:, (i % 4) * D:(i % 4 + 1) * D],
                            ALU.mult, ALU.mult,
                            accum_out=SQ[:, i:i + 1]))
                        vector.tensor_scalar(E2E[:, i:i + 1], SQ[:, i:i + 1],
                                             INV_D, EPS, ALU.mult, ALU.add)
                        vector.tensor_scalar(MU[:, i:i + 1], SU[:, i:i + 1],
                                             INV_D, None, ALU.mult)
                        vb(vector.tensor_scalar(NEGMU[:, i:i + 1], SU[:, i:i + 1],
                                                -INV_D, None, ALU.mult))
                        vb(vector.scalar_tensor_tensor(VAR[:, i:i + 1],
                                                       MU[:, i:i + 1],
                                                       NEGMU[:, i:i + 1],
                                                       E2E[:, i:i + 1],
                                                       ALU.mult, ALU.add))
                        vi = VAR[:, i:i + 1].bitcast(mybir.dt.int32)
                        vb(vector.tensor_scalar(NTI1[:, i:i + 1], vi, 1, None,
                                                ALU.arith_shift_right))
                        vb(vector.tensor_scalar(NTI2[:, i:i + 1], NTI1[:, i:i + 1],
                                                -1, None, ALU.mult))
                        vb(vector.tensor_scalar(NTIY[:, i:i + 1], NTI2[:, i:i + 1],
                                                0x5f3759df, None, ALU.add))
                        ys = [NTIY[:, i:i + 1].bitcast(F32),
                              NTYa[:, i:i + 1], NTYb[:, i:i + 1]]
                        cur = ys[0]
                        for k in range(3):
                            nxt = ys[1 + (k % 2)] if k < 2 else NTYa[:, i:i + 1]
                            if k == 1:
                                nxt = NTYb[:, i:i + 1]
                            if k == 2:
                                nxt = NTYa[:, i:i + 1]
                            vb(vector.tensor_tensor(NTA[:, i:i + 1], cur, cur, ALU.mult))
                            vb(vector.tensor_tensor(NTB[:, i:i + 1], NTA[:, i:i + 1],
                                                    VAR[:, i:i + 1], ALU.mult))
                            vb(vector.tensor_scalar(NTH[:, i:i + 1], NTB[:, i:i + 1],
                                                    -0.5, 1.5, ALU.mult, ALU.add))
                            vb(vector.tensor_tensor(nxt, cur, NTH[:, i:i + 1], ALU.mult))
                            cur = nxt
                        vector.tensor_tensor(NMS2[:, i:i + 1],
                                             NEGMU[:, i:i + 1], cur,
                                             ALU.mult).then_inc(s_vnm2, 1)
                        vector.wait_ge(s_vnm2, i + 1)
                        if i >= 2:
                            vector.wait_ge(s_vq, i - 1)
                        vb(vector.tensor_scalar(zbuf[:, (i % 2) * D:(i % 2 + 1) * D],
                                                tbuf[:, (i % 4) * D:(i % 4 + 1) * D],
                                                NTYa[:, i:i + 1], NMS2[:, i:i + 1],
                                                ALU.mult, ALU.add))
                        vector.wait_ge(s_gbc, 16)
                        vector.tensor_tensor(qbuf[:, (i % 4) * D:(i % 4 + 1) * D],
                                             zbuf[:, (i % 2) * D:(i % 2 + 1) * D],
                                             gBC[:], ALU.mult).then_inc(s_vq, 1)

                    vector.wait_ge(s_abc, 1)
                    vector.tensor_copy(a2BC[:], pA[:]).then_inc(s_abc, 1)
                    vector.wait_ge(s_cbc, 32)
                    vector.tensor_tensor(CBC[:], ubuf[:, 0:D], wbuf[:, 0:D],
                                         ALU.add).then_inc(s_cbc, 1)
                    chain(0)
                    chain(1)
                    for slot in range(NT + 3):
                        if slot + 2 < NT:
                            chain(slot + 2)
                        i = slot
                        if i < NT:
                            for hf in range(2):
                                hidx = 2 * i + hf
                                vector.wait_ge(s_mm2, hidx + 1)
                                vector.tensor_tensor(
                                    ubuf[:, (i % 2) * D + hf * 512:
                                            (i % 2) * D + hf * 512 + 512],
                                    po[hidx % 2][:],
                                    a2BC[:, hf * 512:(hf + 1) * 512],
                                    ALU.mult).then_inc(s_epu, 1)
                        i2 = slot - 2
                        if 0 <= i2 < NT:
                            vector.wait_ge(s_jr[i2 % 6], 16 * (i2 // 6 + 1))
                            vector.tensor_tensor(wbuf[:, (i2 % 2) * D:(i2 % 2 + 1) * D],
                                                 vbuf[:],
                                                 Jr[:, (i2 % 6) * D:(i2 % 6 + 1) * D],
                                                 ALU.add).then_inc(s_vw, 1)
                        i1 = slot - 1
                        if 0 <= i1 < NT:
                            vector.wait_ge(s_vq, i1 + 1)
                            vector.tensor_tensor(vbuf[:],
                                                 ubuf[:, (i1 % 2) * D:(i1 % 2 + 1) * D],
                                                 qbuf[:, (i1 % 4) * D:(i1 % 4 + 1) * D],
                                                 ALU.add).then_inc(s_epv, 1)
                        i3 = slot - 3
                        if 0 <= i3 < NT:
                            if i3 >= 2:
                                vector.wait_ge(s_ow[i3 % 2], 16 * (i3 // 2))
                            vector.tensor_tensor(ostg[:, (i3 % 2) * D:(i3 % 2 + 1) * D],
                                                 wbuf[:, (i3 % 2) * D:(i3 % 2 + 1) * D],
                                                 CBC[:], ALU.add).then_inc(s_vo, 1)

                @blk.scalar
                def _(scalar):
                    scalar.wait_ge(s_al1, 1)
                    scalar.wait_ge(s_b1l, 16)
                    for nb in range(NBLK):
                        for ht in range(HT):
                            g_idx = nb * HT + ht
                            scalar.wait_ge(s_mm1, g_idx + 1)
                            a1c = (ht % 4) * 8 + ht // 4
                            scalar.activation(G[:, ht * NB:(ht + 1) * NB],
                                              pg[g_idx % 4][:], ACT.Gelu,
                                              bias=b1P[:, ht:ht + 1],
                                              scale=alpha1P[:, a1c:a1c + 1]).then_inc(s_ge, 1)

                @blk.tensor
                def _(tensor):
                    def pa_mm(i):
                        if i >= 1:
                            tensor.wait_ge(s_vc1, i)
                        for hf in range(2):
                            mm = tensor.matmul(
                                pA[:, hf * 512:(hf + 1) * 512],
                                QT[:, i * 128:(i + 1) * 128],
                                Msb[:, hf * 512:(hf + 1) * 512],
                                start=True, stop=True, skip_group_check=True)
                        mm.then_inc(s_teC, 1)

                    tensor.wait_ge(s_sg, 16)
                    for nb in range(NBLK):
                        xt0 = (nb % 2) * 2048
                        tensor.wait_ge(s_trD[nb % 2], 16 * 8 * (nb // 2 + 1))
                        for ht in range(HT):
                            g_idx = nb * HT + ht
                            if nb == 0 and ht == 8:
                                tensor.wait_ge(s_a2l, 16)
                                for hf in range(2):
                                    mm = tensor.matmul(
                                        pA[:, hf * 512:(hf + 1) * 512],
                                        onesF[:], a2row[:1, hf * 512:(hf + 1) * 512],
                                        start=True, stop=True, skip_group_check=True)
                                mm.then_inc(s_abc, 1)
                            if nb == 0 and ht == 12:
                                tensor.wait_ge(s_abc, 2)
                                tensor.wait_ge(s_mev, 1)
                                pa_mm(0)
                            if ht == 2 and nb >= 1 and 2 * nb + 1 < NT:
                                pa_mm(2 * nb + 1)
                            if ht == 16 and nb >= 1 and 2 * nb + 2 < NT:
                                pa_mm(2 * nb + 2)
                            if nb == 0 and ht == 20:
                                pa_mm(1)
                            if nb == 0 and ht == 24:
                                pa_mm(2)
                            if g_idx >= 4:
                                tensor.wait_ge(s_ge, g_idx - 3)
                            for dc in range(DC):
                                mm = tensor.matmul(
                                    pg[g_idx % 4][:],
                                    S1T[:, dc * H + ht * 128: dc * H + ht * 128 + 128],
                                    XTD[:, xt0 + dc * 256: xt0 + dc * 256 + 256],
                                    start=(dc == 0), stop=(dc == 7),
                                    skip_group_check=True)
                            mm.then_inc(s_mm1, 1)
                        tensor.wait_ge(s_ge, HT * (nb + 1))
                        for st in range(NSUB):
                            i = nb * NSUB + st
                            for hf in range(2):
                                hidx = 2 * i + hf
                                if hidx >= 2:
                                    tensor.wait_ge(s_epu, hidx - 1)
                                for hc in range(HT):
                                    mm = tensor.matmul(
                                        po[hidx % 2][:],
                                        G[:, hc * NB + st * 128: hc * NB + st * 128 + 128],
                                        S2T[:, hc * D + hf * 512: hc * D + hf * 512 + 512],
                                        start=(hc == 0), stop=(hc == HT - 1),
                                        skip_group_check=True)
                                mm.then_inc(s_mm2, 1)

    return nc


_NC_CACHE = None


def kernel(L0_attn_J, L0_attn_W_Q, L0_attn_W_K, L0_attn_gamma, L0_attn_beta,
           L0_ffn_w1, L0_ffn_b1, L0_ffn_w2, L0_ffn_b2):
    J = np.ascontiguousarray(np.asarray(L0_attn_J, dtype=np.float32))
    wq = np.asarray(L0_attn_W_Q, dtype=np.float32)
    wk = np.asarray(L0_attn_W_K, dtype=np.float32)
    gamma = np.asarray(L0_attn_gamma, dtype=np.float32)
    beta = np.asarray(L0_attn_beta, dtype=np.float32)
    w1 = np.ascontiguousarray(np.asarray(L0_ffn_w1, dtype=np.float32))
    b1 = np.asarray(L0_ffn_b1, dtype=np.float32)
    w2 = np.ascontiguousarray(np.asarray(L0_ffn_w2, dtype=np.float32))
    b2 = np.asarray(L0_ffn_b2, dtype=np.float32)

    wqT = np.ascontiguousarray(wq.T)
    wkT = np.ascontiguousarray(wk.T)
    w1T = np.ascontiguousarray(w1.T)
    w2T = np.ascontiguousarray(w2.T)
    gB = np.ascontiguousarray(np.broadcast_to(gamma[None, :], (P, D)))
    bB = np.ascontiguousarray(np.broadcast_to(beta[None, :], (P, D)))
    b2B = np.ascontiguousarray(np.broadcast_to(b2[None, :], (P, D)))
    b1P = np.ascontiguousarray(b1.reshape(32, 128).T)
    ident = np.eye(P, dtype=np.float32)

    global _NC_CACHE
    if _NC_CACHE is None:
        _NC_CACHE = build_nc()
    nc = _NC_CACHE

    in_maps = []
    for b in range(8):
        in_maps.append({
            "J": np.ascontiguousarray(J[b]),
            "wqT": wqT, "wkT": wkT, "w1T": w1T, "w2T": w2T,
            "w1n": w1, "w2n": w2,
            "gB": gB, "bB": bB, "b2B": b2B, "b1P": b1P,
            "iden": ident,
        })
    res = run_bass_kernel_spmd(nc, in_maps, core_ids=list(range(8)))
    out = np.stack([np.asarray(r["out"], dtype=np.float32) for r in res.results], axis=0)
    return out


# version 13 (final)
# speedup vs baseline: 1.0815x; 1.0339x over previous
import math
import numpy as np
from contextlib import ExitStack

import concourse.bass as bass
import concourse.mybir as mybir
from concourse.bass_utils import run_bass_kernel_spmd

F32 = mybir.dt.float32
BF16 = mybir.dt.bfloat16
FP8 = mybir.dt.float8e4
ALU = mybir.AluOpType
ACT = mybir.ActivationFunctionType
AX = mybir.AxisListType.X

P = 128
D = 1024
H = 4096
N = 4096
NCH = 32
DC = 8
HT = 32
NB = 256
NBLK = 16
NSUB = 2
NT = NBLK * NSUB
INV_D = 1.0 / 1024.0
INV_H = 1.0 / 4096.0
EPS = 1e-5
ISQ = 1.0 / math.sqrt(128.0)


def build_nc():
    nc = bass.Bass("TRN2", target_bir_lowering=False, detect_race_conditions=False)

    dJ = nc.dram_tensor("J", [N, D], F32, kind="ExternalInput")
    dwqT = nc.dram_tensor("wqT", [D, 128], F32, kind="ExternalInput")
    dwkT = nc.dram_tensor("wkT", [D, 128], F32, kind="ExternalInput")
    dw1T = nc.dram_tensor("w1T", [D, H], F32, kind="ExternalInput")
    dw2T = nc.dram_tensor("w2T", [H, D], F32, kind="ExternalInput")
    dw1n = nc.dram_tensor("w1n", [H, D], F32, kind="ExternalInput")
    dw2n = nc.dram_tensor("w2n", [D, H], F32, kind="ExternalInput")
    dgB = nc.dram_tensor("gB", [P, D], F32, kind="ExternalInput")
    dbB = nc.dram_tensor("bB", [P, D], F32, kind="ExternalInput")
    db2B = nc.dram_tensor("b2B", [P, D], F32, kind="ExternalInput")
    db1P = nc.dram_tensor("b1P", [P, 32], F32, kind="ExternalInput")
    diden = nc.dram_tensor("iden", [P, P], F32, kind="ExternalInput")
    dout = nc.dram_tensor("out", [N, D], F32, kind="ExternalOutput")
    dxnat = nc.dram_tensor("xnat", [N, D], BF16, kind="Internal")
    da2 = nc.dram_tensor("a2scr", [D], F32, kind="Internal")

    ctx = ExitStack()
    sb = lambda n, s, d: ctx.enter_context(nc.sbuf_tensor(n, s, d))
    sem = lambda n: ctx.enter_context(nc.semaphore(n))

    with ctx:
        S1T = sb("S1T", [P, DC * H], FP8)
        S2T = sb("S2T", [P, HT * D], FP8)
        QT = sb("QT", [P, N], BF16)
        Msb = sb("Msb", [P, D], BF16)
        alpha1P = sb("alpha1P", [P, 32], F32)
        AL1 = sb("AL1", [P, 32], F32)
        AL2p = sb("AL2p", [P, 32], F32)
        AL2q = sb("AL2q", [P, 16], F32)
        AL2r = sb("AL2r", [P, 8], F32)
        AL2s = sb("AL2s", [P, 8], F32)
        b1P = sb("b1Ps", [P, 32], F32)
        onesF = sb("onesF", [1, P], F32)
        junk1 = sb("junk1", [P, 8], F32)
        jctr = [0]
        SU = sb("SU", [P, 32], F32)
        SQ = sb("SQ", [P, 32], F32)
        MU = sb("MU", [P, 32], F32)
        E2E = sb("E2E", [P, 32], F32)
        NEGMU = sb("NEGMU", [P, 32], F32)
        VAR = sb("VAR", [P, 32], F32)
        REC = sb("REC", [P, 32], F32)
        RS = sb("RS", [P, 32], F32)
        NMS = sb("NMS", [P, 32], F32)
        RS2 = sb("RS2", [P, 32], F32)
        NMS2 = sb("NMS2", [P, 32], F32)
        dumpA = sb("dumpA", [P, D], F32)
        XTD = sb("XTD", [P, 2 * 2048], BF16)
        NTI1 = sb("NTI1", [P, 32], mybir.dt.int32)
        NTI2 = sb("NTI2", [P, 32], mybir.dt.int32)
        NTIY = sb("NTIY", [P, 32], mybir.dt.int32)
        NTYa = sb("NTYa", [P, 32], F32)
        NTYb = sb("NTYb", [P, 32], F32)
        NTA = sb("NTA", [P, 32], F32)
        NTB = sb("NTB", [P, 32], F32)
        NTH = sb("NTH", [P, 32], F32)

        s_j = [sem(f"s_j{k}") for k in range(10)]
        s_xw = [sem(f"s_xw{k}") for k in range(3)]
        s_sq = sem("s_sq"); s_vrec = sem("s_vrec"); s_rs = sem("s_rs")
        s_vnm = sem("s_vnm"); s_xn = sem("s_xn")
        s_wt = [sem(f"s_wt{k}") for k in range(2)]
        s_sg = sem("s_sg")
        s_wn = [sem(f"s_wn{k}") for k in range(2)]
        s_ab = sem("s_ab"); s_al1 = sem("s_al1"); s_al2 = sem("s_al2")
        s_b1l = sem("s_b1l"); s_a2w = sem("s_a2w"); s_a2l = sem("s_a2l")
        s_wqk = sem("s_wqk")
        s_trB = [sem(f"s_trB{k}") for k in range(2)]
        s_teT = sem("s_teT"); s_evX = sem("s_evX"); s_idl = sem("s_idl")
        s_xc = [sem(f"s_xc{k}") for k in range(4)]
        s_teK = sem("s_teK"); s_kev = sem("s_kev"); s_teMM = sem("s_teMM")
        s_teQ = sem("s_teQ"); s_qev = sem("s_qev"); s_mev = sem("s_mev")
        s_jr = [sem(f"s_jr{k}") for k in range(6)]
        s_cb = sem("s_cb")
        cbn = [0]
        s_xr = sem("s_xr"); s_vc1 = sem("s_vc1"); s_sq2 = sem("s_sq2")
        s_vrec2 = sem("s_vrec2"); s_rs2 = sem("s_rs2"); s_vnm2 = sem("s_vnm2")
        s_z = sem("s_z"); s_teC = sem("s_teC"); s_vq = sem("s_vq")
        s_gbc = sem("s_gbc"); s_cbc = sem("s_cbc"); s_abc = sem("s_abc")
        s_trD = [sem(f"s_trD{k}") for k in range(2)]
        s_mm1 = sem("s_mm1"); s_ge = sem("s_ge"); s_mm2 = sem("s_mm2")
        s_epu = sem("s_epu"); s_epv = sem("s_epv"); s_vw = sem("s_vw")
        s_vo = sem("s_vo")
        s_ow = [sem(f"s_ow{k}") for k in range(2)]


        wg_slot = {3 + 2 * g: g for g in range(16)}
        wn_slot = {4 + 2 * u: u for u in range(16)}

        with ExitStack() as sA:
            Jring = sA.enter_context(nc.sbuf_tensor("Jring", [P, 10 * D], BF16))
            XNr = sA.enter_context(nc.sbuf_tensor("XNr", [P, 3 * D], BF16))
            wTbf = sA.enter_context(nc.sbuf_tensor("wTbf", [P, 2 * 4096], BF16))
            sgnb = sA.enter_context(nc.sbuf_tensor("sgnb", [P, 4096], BF16))
            wnb = sA.enter_context(nc.sbuf_tensor("wnb", [P, 2 * 4096], BF16))
            XTe = sA.enter_context(nc.sbuf_tensor("XTe", [P, 4 * D], BF16))
            IDN = sA.enter_context(nc.sbuf_tensor("IDN", [P, P], BF16))
            XNc = sA.enter_context(nc.sbuf_tensor("XNc", [P, 4 * D], BF16))
            WQTb = sA.enter_context(nc.sbuf_tensor("WQTb", [P, D], BF16))
            WKTb = sA.enter_context(nc.sbuf_tensor("WKTb", [P, D], BF16))
            Kb = sA.enter_context(nc.sbuf_tensor("Kb", [P, 8 * 128], BF16))
            pK = [sA.enter_context(nc.psum_tensor(f"pK{x}", [P, 128], F32)) for x in range(2)]
            pQT = [sA.enter_context(nc.psum_tensor(f"pQT{x}", [P, 256], F32)) for x in range(2)]
            pM = sA.enter_context(nc.psum_tensor("pM", [P, D], F32))
            pT = sA.enter_context(nc.psum_tensor("pT", [P, D], F32))

            with nc.Block() as blk:
                @blk.gpsimd
                def _(gpsimd):
                    gpsimd.dma_start(IDN[:], diden.ap()).then_inc(s_idl, 16)

                    def wt_load(g):
                        base = (g % 2) * 4096
                        if g < 8:
                            gpsimd.dma_start(wTbf[:, base:base + 4096],
                                             dw1T[g * 128:(g + 1) * 128, :]
                                             ).then_inc(s_wt[g % 2], 16)
                        else:
                            t = g - 8
                            gpsimd.dma_start(
                                wTbf[:, base:base + 4096]
                                .rearrange("p (a b) -> p a b", b=D),
                                bass.AP(dw2T, t * 512 * D,
                                        [[D, 128], [128 * D, 4], [1, D]])
                                ).then_inc(s_wt[g % 2], 16)

                    def wn_load(u):
                        base = (u % 2) * 4096
                        if u < 8:
                            gpsimd.dma_start(
                                wnb[:, base:base + 4096]
                                .rearrange("p (a b) -> p a b", b=D),
                                bass.AP(dw1n, u * 512 * D,
                                        [[D, 128], [128 * D, 4], [1, D]])
                                ).then_inc(s_wn[u % 2], 16)
                        else:
                            t = u - 8
                            gpsimd.dma_start(wnb[:, base:base + 4096],
                                             dw2n[t * 128:(t + 1) * 128, :]
                                             ).then_inc(s_wn[u % 2], 16)

                    wt_load(0)
                    wt_load(1)
                    wn_load(0)
                    wn_load(1)
                    for c in range(6):
                        gpsimd.dma_start(Jring[:, (c % 10) * D:(c % 10 + 1) * D],
                                         dJ[c * 128:(c + 1) * 128, :]).then_inc(s_j[c % 10], 16)
                    for dc in range(DC):
                        gpsimd.dma_start(WQTb[:, dc * 128:(dc + 1) * 128],
                                         dwqT[dc * 128:(dc + 1) * 128, :]).then_inc(s_wqk, 16)
                    for dc in range(DC):
                        gpsimd.dma_start(WKTb[:, dc * 128:(dc + 1) * 128],
                                         dwkT[dc * 128:(dc + 1) * 128, :]).then_inc(s_wqk, 16)
                    for c in range(41):
                        if 6 <= c + 4 < NCH:
                            if c - 5 >= 1:
                                gpsimd.wait_ge(s_xn, c - 5)
                            gpsimd.dma_start(Jring[:, ((c + 4) % 10) * D:((c + 4) % 10 + 1) * D],
                                             dJ[(c + 4) * 128:(c + 5) * 128, :]
                                             ).then_inc(s_j[(c + 4) % 10], 16)
                        cx = c - 10
                        if 0 <= cx <= 29:
                            gpsimd.wait_ge(s_xw[cx % 3], 16 * (cx // 3 + 1))
                            if cx >= 4:
                                gpsimd.wait_ge(s_teMM, cx - 3)
                            gpsimd.dma_start(XNc[:, (cx % 4) * D:(cx % 4 + 1) * D],
                                             dxnat[cx * 128:(cx + 1) * 128, :]
                                             ).then_inc(s_xc[cx % 4], 16)
                        if c in wg_slot and wg_slot[c] + 2 < 16:
                            g = wg_slot[c] + 2
                            gpsimd.wait_ge(s_sg, g - 1)
                            wt_load(g)
                        if c in wn_slot and wn_slot[c] + 2 < 16:
                            u = wn_slot[c] + 2
                            gpsimd.wait_ge(s_ab, 4 * (u - 1))
                            wn_load(u)
                    for cx in range(30, NCH):
                        gpsimd.wait_ge(s_xw[cx % 3], 16 * (cx // 3 + 1))
                        gpsimd.wait_ge(s_teMM, cx - 3)
                        gpsimd.dma_start(XNc[:, (cx % 4) * D:(cx % 4 + 1) * D],
                                         dxnat[cx * 128:(cx + 1) * 128, :]
                                         ).then_inc(s_xc[cx % 4], 16)

                @blk.sync
                def _(sync):
                    sync.dma_start(b1P[:], db1P.ap()).then_inc(s_b1l, 16)
                    for c in range(41):
                        if 0 <= c - 8 < NCH:
                            cc = c - 8
                            sync.wait_ge(s_xn, cc + 1)
                            sync.dma_start(dxnat[cc * 128:(cc + 1) * 128, :],
                                           XNr[:, (cc % 3) * D:(cc % 3 + 1) * D]
                                           ).then_inc(s_xw[cc % 3], 16)
                        if c in (16, 24):
                            q = {16: 0, 24: 1}[c]
                            for k in range(3):
                                ndone = 2 * (q + 1)
                                cnt = (ndone - 1 - k) // 3 + 1 if ndone - 1 >= k else 0
                                if cnt > 0:
                                    sync.wait_ge(s_xw[k], 16 * cnt)
                            for dc in range(DC):
                                sync.dma_start(
                                    XTD[:, (q % 2) * 2048 + dc * 256:
                                           (q % 2) * 2048 + dc * 256 + 256],
                                    bass.AP(dxnat, q * 256 * D + dc * 128,
                                            [[D, 256], [1, 128]]),
                                    transpose=True).then_inc(s_trD[q % 2], 16)

                @blk.vector
                def _(vector):
                    vector.memset(onesF[:], 1.0)
                    for c in range(41):
                        if c < NCH:
                            vector.wait_ge(s_j[c % 10], 16 * (c // 10 + 1))
                            vector.reduce_sum(SU[:, c:c + 1],
                                              Jring[:, (c % 10) * D:(c % 10 + 1) * D],
                                              axis=AX)
                        if c in wg_slot:
                            g = wg_slot[c]
                            vector.wait_ge(s_wt[g % 2], 16 * (g // 2 + 1))
                            vector.tensor_scalar(sgnb[:], wTbf[:, (g % 2) * 4096:
                                                               (g % 2) * 4096 + 4096],
                                                 1e38, 1.0, ALU.mult, ALU.min)
                        if 0 <= c - 2 < NCH:
                            vector.wait_ge(s_sq, c - 1)
                            vector.tensor_scalar(E2E[:, c - 2:c - 1], SQ[:, c - 2:c - 1],
                                                 INV_D, EPS, ALU.mult, ALU.add)
                            vector.tensor_scalar(MU[:, c - 2:c - 1], SU[:, c - 2:c - 1],
                                                 INV_D, None, ALU.mult)
                            vector.tensor_scalar(NEGMU[:, c - 2:c - 1], SU[:, c - 2:c - 1],
                                                 -INV_D, None, ALU.mult)
                        elif c in wg_slot:
                            vector.memset(junk1[:, jctr[0] % 8:jctr[0] % 8 + 1], 0.0)
                            jctr[0] += 1
                            vector.memset(junk1[:, jctr[0] % 8:jctr[0] % 8 + 1], 0.0)
                            jctr[0] += 1
                        if c in wg_slot:
                            g = wg_slot[c]
                            if g < 8:
                                dst = S1T[:, g * H: g * H + 4096]
                            else:
                                dst = S2T[:, (g - 8) * 4096: (g - 8) * 4096 + 4096]
                            vector.tensor_scalar(dst, sgnb[:], -1.0, None,
                                                 ALU.max).then_inc(s_sg, 1)
                        if 0 <= c - 3 < NCH:
                            vector.scalar_tensor_tensor(VAR[:, c - 3:c - 2],
                                                        MU[:, c - 3:c - 2],
                                                        NEGMU[:, c - 3:c - 2],
                                                        E2E[:, c - 3:c - 2],
                                                        ALU.mult, ALU.add)
                        if 0 <= c - 4 < NCH:
                            vector.reciprocal(REC[:, c - 4:c - 3],
                                              VAR[:, c - 4:c - 3]).then_inc(s_vrec, 1)
                        if 0 <= c - 6 < NCH:
                            vector.wait_ge(s_rs, c - 5)
                            vector.tensor_tensor(NMS[:, c - 6:c - 5],
                                                 NEGMU[:, c - 6:c - 5],
                                                 RS[:, c - 6:c - 5],
                                                 ALU.mult).then_inc(s_vnm, 1)
                        if 0 <= c - 7 < NCH:
                            cc = c - 7
                            if cc >= 3:
                                vector.wait_ge(s_xw[cc % 3], 16 * (cc // 3))
                                vector.wait_ge(s_teT, cc - 2)
                            pass
                            vector.tensor_scalar(XNr[:, (cc % 3) * D:(cc % 3 + 1) * D],
                                                 Jring[:, (cc % 10) * D:(cc % 10 + 1) * D],
                                                 RS[:, cc:cc + 1], NMS[:, cc:cc + 1],
                                                 ALU.mult, ALU.add).then_inc(s_xn, 1)
                        if 0 <= c - 8 < NCH:
                            ec = c - 8
                            vector.wait_ge(s_teT, ec + 1)
                            if ec >= 4:
                                vector.wait_ge(s_teQ, (ec - 4) // 2 + 1)
                            vector.tensor_copy(XTe[:, (ec % 4) * D:(ec % 4 + 1) * D],
                                               pT[:]).then_inc(s_evX, 1)
                        if c == 21:
                            vector.wait_ge(s_ab, 32)
                            vector.tensor_scalar(alpha1P[:], AL1[:], INV_D, None,
                                                 ALU.mult).then_inc(s_al1, 1)
                        if 10 <= c <= 40:
                            idx = c - 10
                            nb, half = idx // 2, idx % 2
                            cch = 2 * nb + half
                            vector.wait_ge(s_teK, cch + 1)
                            if cch >= 8:
                                vector.wait_ge(s_teMM, cch - 7)
                            vector.tensor_copy(Kb[:, (cch % 8) * 128:(cch % 8 + 1) * 128],
                                               pK[cch % 2][:]).then_inc(s_kev, 1)
                            if half == 1:
                                vector.wait_ge(s_teQ, nb + 1)
                                vector.tensor_copy(QT[:, nb * 256:(nb + 1) * 256],
                                                   pQT[nb % 2][:]).then_inc(s_qev, 1)
                        if c == 36:
                            vector.wait_ge(s_ab, 64)
                            vector.tensor_tensor(AL2q[:, 0:8], AL2p[:, 0:8],
                                                 AL2p[:, 8:16], ALU.add)
                        if c == 37:
                            vector.tensor_tensor(AL2q[:, 8:16], AL2p[:, 16:24],
                                                 AL2p[:, 24:32], ALU.add)
                        if c == 38:
                            vector.tensor_tensor(AL2r[:], AL2q[:, 0:8],
                                                 AL2q[:, 8:16], ALU.add)
                        if c == 39:
                            vector.tensor_scalar(AL2s[:], AL2r[:], INV_H, None,
                                                 ALU.mult).then_inc(s_al2, 1)
                    vector.wait_ge(s_teK, NCH)
                    vector.wait_ge(s_teMM, 24)
                    vector.tensor_copy(Kb[:, (31 % 8) * 128:(31 % 8 + 1) * 128],
                                       pK[31 % 2][:]).then_inc(s_kev, 1)
                    vector.wait_ge(s_teQ, 16)
                    vector.tensor_copy(QT[:, 15 * 256:16 * 256],
                                       pQT[15 % 2][:]).then_inc(s_qev, 1)
                    vector.wait_ge(s_teMM, NCH)
                    vector.tensor_copy(Msb[:], pM[:]).then_inc(s_mev, 1)

                @blk.scalar
                def _(scalar):
                    for c in range(41):
                        if 0 <= c - 5 < NCH:
                            scalar.wait_ge(s_vrec, c - 4)
                            scalar.sqrt(RS[:, c - 5:c - 4],
                                        REC[:, c - 5:c - 4]).then_inc(s_rs, 1)
                        if c < NCH:
                            scalar.wait_ge(s_j[c % 10], 16 * (c // 10 + 1))
                            scalar.activation(dumpA[:],
                                              Jring[:, (c % 10) * D:(c % 10 + 1) * D],
                                              ACT.Square,
                                              accum_out=SQ[:, c:c + 1]).then_inc(s_sq, 1)
                        if c in wn_slot:
                            u = wn_slot[c]
                            scalar.wait_ge(s_wn[u % 2], 16 * (u // 2 + 1))
                            base = (u % 2) * 4096
                            for k in range(4):
                                if u < 8:
                                    t = 4 * u + k
                                    tgt = AL1[:, (t % 4) * 8 + t // 4:
                                                 (t % 4) * 8 + t // 4 + 1]
                                else:
                                    tt = u - 8
                                    tgt = AL2p[:, k * 8 + tt: k * 8 + tt + 1]
                                scalar.activation(dumpA[:],
                                                  wnb[:, base + k * D:base + (k + 1) * D],
                                                  ACT.Abs, accum_out=tgt).then_inc(s_ab, 1)

                @blk.tensor
                def _(tensor):
                    tensor.wait_ge(s_wqk, 16 * 16)
                    tensor.wait_ge(s_idl, 16)
                    LM = 6

                    def do_m(mcc):
                        tensor.wait_ge(s_kev, mcc + 1)
                        tensor.wait_ge(s_xc[mcc % 4], 16 * (mcc // 4 + 1))
                        for hf in range(2):
                            mm = tensor.matmul(
                                pM[:, hf * 512:(hf + 1) * 512],
                                Kb[:, (mcc % 8) * 128:(mcc % 8 + 1) * 128],
                                XNc[:, (mcc % 4) * D + hf * 512:
                                       (mcc % 4) * D + hf * 512 + 512],
                                start=(mcc == 0), stop=(mcc == NCH - 1),
                                skip_group_check=True)
                        mm.then_inc(s_teMM, 1)

                    for cch in range(NCH):
                        tensor.wait_ge(s_xn, cch + 1)
                        if cch >= 1:
                            tensor.wait_ge(s_evX, cch)
                        for dc in range(DC):
                            mm = tensor.matmul(
                                pT[:, dc * 128:(dc + 1) * 128],
                                XNr[:, (cch % 3) * D + dc * 128:
                                       (cch % 3) * D + dc * 128 + 128],
                                IDN[:], start=True, stop=True,
                                skip_group_check=True)
                        mm.then_inc(s_teT, 1)
                        tensor.wait_ge(s_evX, cch + 1)
                        if cch >= 2:
                            tensor.wait_ge(s_kev, cch - 1)
                        for dc in range(DC):
                            mm = tensor.matmul(
                                pK[cch % 2][:],
                                XTe[:, (cch % 4) * D + dc * 128:
                                       (cch % 4) * D + dc * 128 + 128],
                                WKTb[:, dc * 128:(dc + 1) * 128],
                                start=(dc == 0), stop=(dc == DC - 1),
                                skip_group_check=True)
                        mm.then_inc(s_teK, 1)
                        if cch >= LM:
                            do_m(cch - LM)
                        if cch % 2 == 1:
                            nb = cch // 2
                            if nb >= 2:
                                tensor.wait_ge(s_qev, nb - 1)
                            for j in range(2):
                                for dc in range(DC):
                                    mm = tensor.matmul(
                                        pQT[nb % 2][:, j * 128:(j + 1) * 128],
                                        WQTb[:, dc * 128:(dc + 1) * 128],
                                        XTe[:, ((2 * nb + j) % 4) * D + dc * 128:
                                               ((2 * nb + j) % 4) * D + dc * 128 + 128],
                                        start=(dc == 0), stop=(dc == DC - 1),
                                        skip_group_check=True)
                            mm.then_inc(s_teQ, 1)
                    for mcc in range(NCH - LM, NCH):
                        do_m(mcc)

        with ExitStack() as sD:
            G = sD.enter_context(nc.sbuf_tensor("G", [P, HT * NB], BF16))
            a2BC = sD.enter_context(nc.sbuf_tensor("a2BC", [P, D], F32))
            CBC = sD.enter_context(nc.sbuf_tensor("CBC", [P, D], F32))
            a2row = sD.enter_context(nc.sbuf_tensor("a2row", [1, D], F32))
            gBC = sD.enter_context(nc.sbuf_tensor("gBC", [P, D], F32))
            Jr = sD.enter_context(nc.sbuf_tensor("Jr", [P, 6 * D], F32))
            xrb = sD.enter_context(nc.sbuf_tensor("xrb", [P, 2 * D], BF16))
            tbuf = sD.enter_context(nc.sbuf_tensor("tbuf", [P, 4 * D], BF16))
            zbuf = sD.enter_context(nc.sbuf_tensor("zbuf", [P, 2 * D], BF16))
            qbuf = sD.enter_context(nc.sbuf_tensor("qbuf", [P, 4 * D], BF16))
            ubuf = sD.enter_context(nc.sbuf_tensor("ubuf", [P, 2 * D], F32))
            vbuf = sD.enter_context(nc.sbuf_tensor("vbuf", [P, D], F32))
            wbuf = sD.enter_context(nc.sbuf_tensor("wbuf", [P, 2 * D], F32))
            ostg = sD.enter_context(nc.sbuf_tensor("ostg", [P, 2 * D], F32))
            pg = [sD.enter_context(nc.psum_tensor(f"pg{x}", [P, NB], F32)) for x in range(4)]
            po = [sD.enter_context(nc.psum_tensor(f"po{x}", [P, 512], F32)) for x in range(2)]
            pA = sD.enter_context(nc.psum_tensor("pA", [P, D], F32))
            with nc.Block() as blk:
                @blk.sync
                def _(sync):
                    sync.dma_start(gBC[:], dgB.ap()).then_inc(s_gbc, 16)
                    sync.dma_start(ubuf[:, 0:D], dbB.ap()).then_inc(s_cbc, 16)
                    sync.dma_start(wbuf[:, 0:D], db2B.ap()).then_inc(s_cbc, 16)
                    for i in range(6):
                        sync.dma_start(Jr[:, (i % 6) * D:(i % 6 + 1) * D],
                                       dJ[i * 128:(i + 1) * 128, :]).then_inc(s_jr[i % 6], 16)
                    sync.wait_ge(s_al2, 1)
                    sync.dma_start(bass.AP(da2, 0, [[1, 128], [128, 8]]), AL2s[:],
                                   allow_slow_non_contiguous=True).then_inc(s_a2w, 16)
                    sync.wait_ge(s_a2w, 16)
                    sync.dma_start(a2row[:], bass.AP(da2, 0, [[1, 1], [1, D]])
                                   ).then_inc(s_a2l, 16)
                    for nb in range(NBLK):
                        if nb + 2 < NBLK:
                            nbp = nb + 2
                            sync.wait_ge(s_mm1, HT * (nbp - 1))
                            for dc in range(DC):
                                sync.dma_start(
                                    XTD[:, (nbp % 2) * 2048 + dc * 256:
                                           (nbp % 2) * 2048 + dc * 256 + 256],
                                    bass.AP(dxnat, nbp * 256 * D + dc * 128,
                                            [[D, 256], [1, 128]]),
                                    transpose=True).then_inc(s_trD[nbp % 2], 16)
                        for st in range(NSUB):
                            i = nb * NSUB + st
                            if i + 6 < NT:
                                sync.wait_ge(s_vw, i + 1)
                                sync.dma_start(Jr[:, ((i + 6) % 6) * D:((i + 6) % 6 + 1) * D],
                                               dJ[(i + 6) * 128:(i + 7) * 128, :]
                                               ).then_inc(s_jr[(i + 6) % 6], 16)
                            if i - 3 >= 0:
                                ii = i - 3
                                sync.wait_ge(s_vo, ii + 1)
                                sync.dma_start(dout[ii * 128:(ii + 1) * 128, :],
                                               ostg[:, (ii % 2) * D:(ii % 2 + 1) * D]
                                               ).then_inc(s_ow[ii % 2], 16)
                    for ii in range(NT - 3, NT):
                        sync.wait_ge(s_vo, ii + 1)
                        sync.dma_start(dout[ii * 128:(ii + 1) * 128, :],
                                       ostg[:, (ii % 2) * D:(ii % 2 + 1) * D]
                                       ).then_inc(s_ow[ii % 2], 16)
                    sync.wait_ge(s_ow[0], 16 * (NT // 2))
                    sync.wait_ge(s_ow[1], 16 * (NT // 2))

                @blk.vector
                def _(vector):
                    def vb(ins):
                        ins.then_inc(s_cb, 1)
                        cbn[0] += 1
                        vector.wait_ge(s_cb, cbn[0])

                    def chain(i):
                        vector.wait_ge(s_jr[i % 6], 16 * (i // 6 + 1))
                        vb(vector.tensor_scalar(xrb[:, (i % 2) * D:(i % 2 + 1) * D],
                                                Jr[:, (i % 6) * D:(i % 6 + 1) * D],
                                                RS[:, i:i + 1], NMS[:, i:i + 1],
                                                ALU.mult, ALU.add))
                        vector.wait_ge(s_teC, i + 1)
                        vector.scalar_tensor_tensor(
                            tbuf[:, (i % 4) * D:(i % 4 + 1) * D],
                            pA[:], ISQ,
                            xrb[:, (i % 2) * D:(i % 2 + 1) * D],
                            ALU.mult, ALU.add,
                            accum_out=SU[:, i:i + 1]).then_inc(s_vc1, 1)
                        vector.wait_ge(s_vc1, i + 1)
                        vb(vector.scalar_tensor_tensor(
                            dumpA[:],
                            tbuf[:, (i % 4) * D:(i % 4 + 1) * D], 1.0,
                            tbuf[:, (i % 4) * D:(i % 4 + 1) * D],
                            ALU.mult, ALU.mult,
                            accum_out=SQ[:, i:i + 1]))
                        vector.tensor_scalar(E2E[:, i:i + 1], SQ[:, i:i + 1],
                                             INV_D, EPS, ALU.mult, ALU.add)
                        vector.tensor_scalar(MU[:, i:i + 1], SU[:, i:i + 1],
                                             INV_D, None, ALU.mult)
                        vb(vector.tensor_scalar(NEGMU[:, i:i + 1], SU[:, i:i + 1],
                                                -INV_D, None, ALU.mult))
                        vb(vector.scalar_tensor_tensor(VAR[:, i:i + 1],
                                                       MU[:, i:i + 1],
                                                       NEGMU[:, i:i + 1],
                                                       E2E[:, i:i + 1],
                                                       ALU.mult, ALU.add))
                        vi = VAR[:, i:i + 1].bitcast(mybir.dt.int32)
                        vb(vector.tensor_scalar(NTI1[:, i:i + 1], vi, 1, None,
                                                ALU.arith_shift_right))
                        vb(vector.tensor_scalar(NTI2[:, i:i + 1], NTI1[:, i:i + 1],
                                                -1, None, ALU.mult))
                        vb(vector.tensor_scalar(NTIY[:, i:i + 1], NTI2[:, i:i + 1],
                                                0x5f3759df, None, ALU.add))
                        ys = [NTIY[:, i:i + 1].bitcast(F32),
                              NTYa[:, i:i + 1], NTYb[:, i:i + 1]]
                        cur = ys[0]
                        for k in range(3):
                            nxt = ys[1 + (k % 2)] if k < 2 else NTYa[:, i:i + 1]
                            if k == 1:
                                nxt = NTYb[:, i:i + 1]
                            if k == 2:
                                nxt = NTYa[:, i:i + 1]
                            vb(vector.tensor_tensor(NTA[:, i:i + 1], cur, cur, ALU.mult))
                            vb(vector.tensor_tensor(NTB[:, i:i + 1], NTA[:, i:i + 1],
                                                    VAR[:, i:i + 1], ALU.mult))
                            vb(vector.tensor_scalar(NTH[:, i:i + 1], NTB[:, i:i + 1],
                                                    -0.5, 1.5, ALU.mult, ALU.add))
                            vb(vector.tensor_tensor(nxt, cur, NTH[:, i:i + 1], ALU.mult))
                            cur = nxt
                        vector.tensor_tensor(NMS2[:, i:i + 1],
                                             NEGMU[:, i:i + 1], cur,
                                             ALU.mult).then_inc(s_vnm2, 1)
                        vector.wait_ge(s_vnm2, i + 1)
                        if i >= 2:
                            vector.wait_ge(s_vq, i - 1)
                        vb(vector.tensor_scalar(zbuf[:, (i % 2) * D:(i % 2 + 1) * D],
                                                tbuf[:, (i % 4) * D:(i % 4 + 1) * D],
                                                NTYa[:, i:i + 1], NMS2[:, i:i + 1],
                                                ALU.mult, ALU.add))
                        vector.wait_ge(s_gbc, 16)
                        vector.tensor_tensor(qbuf[:, (i % 4) * D:(i % 4 + 1) * D],
                                             zbuf[:, (i % 2) * D:(i % 2 + 1) * D],
                                             gBC[:], ALU.mult).then_inc(s_vq, 1)

                    vector.wait_ge(s_abc, 1)
                    vector.tensor_copy(a2BC[:], pA[:]).then_inc(s_abc, 1)
                    vector.wait_ge(s_cbc, 32)
                    vector.tensor_tensor(CBC[:], ubuf[:, 0:D], wbuf[:, 0:D],
                                         ALU.add).then_inc(s_cbc, 1)
                    chain(0)
                    chain(1)
                    for slot in range(NT + 3):
                        if slot + 2 < NT:
                            chain(slot + 2)
                        i = slot
                        if i < NT:
                            for hf in range(2):
                                hidx = 2 * i + hf
                                vector.wait_ge(s_mm2, hidx + 1)
                                vector.tensor_tensor(
                                    ubuf[:, (i % 2) * D + hf * 512:
                                            (i % 2) * D + hf * 512 + 512],
                                    po[hidx % 2][:],
                                    a2BC[:, hf * 512:(hf + 1) * 512],
                                    ALU.mult).then_inc(s_epu, 1)
                        i2 = slot - 2
                        if 0 <= i2 < NT:
                            vector.wait_ge(s_jr[i2 % 6], 16 * (i2 // 6 + 1))
                            vector.tensor_tensor(wbuf[:, (i2 % 2) * D:(i2 % 2 + 1) * D],
                                                 vbuf[:],
                                                 Jr[:, (i2 % 6) * D:(i2 % 6 + 1) * D],
                                                 ALU.add).then_inc(s_vw, 1)
                        i1 = slot - 1
                        if 0 <= i1 < NT:
                            vector.wait_ge(s_vq, i1 + 1)
                            vector.tensor_tensor(vbuf[:],
                                                 ubuf[:, (i1 % 2) * D:(i1 % 2 + 1) * D],
                                                 qbuf[:, (i1 % 4) * D:(i1 % 4 + 1) * D],
                                                 ALU.add).then_inc(s_epv, 1)
                        i3 = slot - 3
                        if 0 <= i3 < NT:
                            if i3 >= 2:
                                vector.wait_ge(s_ow[i3 % 2], 16 * (i3 // 2))
                            vector.tensor_tensor(ostg[:, (i3 % 2) * D:(i3 % 2 + 1) * D],
                                                 wbuf[:, (i3 % 2) * D:(i3 % 2 + 1) * D],
                                                 CBC[:], ALU.add).then_inc(s_vo, 1)

                @blk.scalar
                def _(scalar):
                    scalar.wait_ge(s_al1, 1)
                    scalar.wait_ge(s_b1l, 16)
                    for nb in range(NBLK):
                        for ht in range(HT):
                            g_idx = nb * HT + ht
                            scalar.wait_ge(s_mm1, g_idx + 1)
                            a1c = (ht % 4) * 8 + ht // 4
                            scalar.activation(G[:, ht * NB:(ht + 1) * NB],
                                              pg[g_idx % 4][:], ACT.Gelu,
                                              bias=b1P[:, ht:ht + 1],
                                              scale=alpha1P[:, a1c:a1c + 1]).then_inc(s_ge, 1)

                @blk.tensor
                def _(tensor):
                    def pa_mm(i):
                        if i >= 1:
                            tensor.wait_ge(s_vc1, i)
                        for hf in range(2):
                            mm = tensor.matmul(
                                pA[:, hf * 512:(hf + 1) * 512],
                                QT[:, i * 128:(i + 1) * 128],
                                Msb[:, hf * 512:(hf + 1) * 512],
                                start=True, stop=True, skip_group_check=True)
                        mm.then_inc(s_teC, 1)

                    tensor.wait_ge(s_sg, 16)
                    for nb in range(NBLK):
                        xt0 = (nb % 2) * 2048
                        tensor.wait_ge(s_trD[nb % 2], 16 * 8 * (nb // 2 + 1))
                        for ht in range(HT):
                            g_idx = nb * HT + ht
                            if nb == 0 and ht == 8:
                                tensor.wait_ge(s_a2l, 16)
                                for hf in range(2):
                                    mm = tensor.matmul(
                                        pA[:, hf * 512:(hf + 1) * 512],
                                        onesF[:], a2row[:1, hf * 512:(hf + 1) * 512],
                                        start=True, stop=True, skip_group_check=True)
                                mm.then_inc(s_abc, 1)
                            if nb == 0 and ht == 12:
                                tensor.wait_ge(s_abc, 2)
                                tensor.wait_ge(s_mev, 1)
                                pa_mm(0)
                            if ht == 2 and nb >= 1 and 2 * nb + 1 < NT:
                                pa_mm(2 * nb + 1)
                            if ht == 16 and nb >= 1 and 2 * nb + 2 < NT:
                                pa_mm(2 * nb + 2)
                            if nb == 0 and ht == 20:
                                pa_mm(1)
                            if nb == 0 and ht == 24:
                                pa_mm(2)
                            if g_idx >= 4:
                                tensor.wait_ge(s_ge, g_idx - 3)
                            for dc in range(DC):
                                mm = tensor.matmul(
                                    pg[g_idx % 4][:],
                                    S1T[:, dc * H + ht * 128: dc * H + ht * 128 + 128],
                                    XTD[:, xt0 + dc * 256: xt0 + dc * 256 + 256],
                                    start=(dc == 0), stop=(dc == 7),
                                    skip_group_check=True)
                            mm.then_inc(s_mm1, 1)
                        tensor.wait_ge(s_ge, HT * (nb + 1))
                        for st in range(NSUB):
                            i = nb * NSUB + st
                            for hf in range(2):
                                hidx = 2 * i + hf
                                if hidx >= 2:
                                    tensor.wait_ge(s_epu, hidx - 1)
                                for hc in range(HT):
                                    mm = tensor.matmul(
                                        po[hidx % 2][:],
                                        G[:, hc * NB + st * 128: hc * NB + st * 128 + 128],
                                        S2T[:, hc * D + hf * 512: hc * D + hf * 512 + 512],
                                        start=(hc == 0), stop=(hc == HT - 1),
                                        skip_group_check=True)
                                mm.then_inc(s_mm2, 1)

    return nc


_NC_CACHE = None


def kernel(L0_attn_J, L0_attn_W_Q, L0_attn_W_K, L0_attn_gamma, L0_attn_beta,
           L0_ffn_w1, L0_ffn_b1, L0_ffn_w2, L0_ffn_b2):
    J = np.ascontiguousarray(np.asarray(L0_attn_J, dtype=np.float32))
    wq = np.asarray(L0_attn_W_Q, dtype=np.float32)
    wk = np.asarray(L0_attn_W_K, dtype=np.float32)
    gamma = np.asarray(L0_attn_gamma, dtype=np.float32)
    beta = np.asarray(L0_attn_beta, dtype=np.float32)
    w1 = np.ascontiguousarray(np.asarray(L0_ffn_w1, dtype=np.float32))
    b1 = np.asarray(L0_ffn_b1, dtype=np.float32)
    w2 = np.ascontiguousarray(np.asarray(L0_ffn_w2, dtype=np.float32))
    b2 = np.asarray(L0_ffn_b2, dtype=np.float32)

    wqT = np.ascontiguousarray(wq.T)
    wkT = np.ascontiguousarray(wk.T)
    w1T = np.ascontiguousarray(w1.T)
    w2T = np.ascontiguousarray(w2.T)
    gB = np.ascontiguousarray(np.broadcast_to(gamma[None, :], (P, D)))
    bB = np.ascontiguousarray(np.broadcast_to(beta[None, :], (P, D)))
    b2B = np.ascontiguousarray(np.broadcast_to(b2[None, :], (P, D)))
    b1P = np.ascontiguousarray(b1.reshape(32, 128).T)
    ident = np.eye(P, dtype=np.float32)

    global _NC_CACHE
    if _NC_CACHE is None:
        _NC_CACHE = build_nc()
    nc = _NC_CACHE

    in_maps = []
    for b in range(8):
        in_maps.append({
            "J": np.ascontiguousarray(J[b]),
            "wqT": wqT, "wkT": wkT, "w1T": w1T, "w2T": w2T,
            "w1n": w1, "w2n": w2,
            "gB": gB, "bB": bB, "b2B": b2B, "b1P": b1P,
            "iden": ident,
        })
    res = run_bass_kernel_spmd(nc, in_maps, core_ids=list(range(8)))
    out = np.stack([np.asarray(r["out"], dtype=np.float32) for r in res.results], axis=0)
    return out
